# version 22
# baseline (speedup 1.0000x reference)
import contextlib
import numpy as np
import ml_dtypes
import concourse.bass as bass
import concourse.mybir as mybir
from concourse.bass_utils import run_bass_kernel_spmd

F32 = mybir.dt.float32
BF16 = mybir.dt.bfloat16
AF = mybir.ActivationFunctionType
ALU = mybir.AluOpType
AX = mybir.AxisListType
BF = ml_dtypes.bfloat16

N = 2048
D = 1024
NCH = 16
DCH = 8
HID = 2730
NU = 22
DT_SZ = {F32: 4, BF16: 2}


def _outer_rope_tables():
    ids = np.arange(N)
    f_ids = ids // 256
    rem = ids - 256 * f_ids
    h_ids = rem // 16
    w_ids = rem - 16 * h_ids
    cos = np.zeros((N, 64), np.float64)
    sin = np.zeros((N, 64), np.float64)
    for c, pos in enumerate((f_ids, h_ids, w_ids)):
        omega = np.arange(10, dtype=np.float64) / 10.0
        inv = 1.0 / 10000.0 ** omega
        freq = pos[:, None].astype(np.float64) * inv
        cos[:, c * 20:(c + 1) * 20] = np.concatenate([np.cos(freq), np.cos(freq)], -1)
        sin[:, c * 20:(c + 1) * 20] = np.concatenate([np.sin(freq), np.sin(freq)], -1)
    cos[:, 60:] = 1.0
    sin[:, 60:] = 0.0
    R = np.zeros((64, 64), np.float64)
    for c in range(3):
        for p in range(10):
            d0, d1 = c * 20 + 2 * p, c * 20 + 2 * p + 1
            R[d0, d1] = -1.0
            R[d1, d0] = 1.0
    return cos, sin, R


def _inner_xpos_tables():
    half_idx = np.arange(0, 64, 2, dtype=np.float64)
    inv_freq = 1.0 / 10000.0 ** (half_idx / 64.0)
    t = np.arange(N, dtype=np.float64)
    freqs = t[:, None] * inv_freq[None, :]
    freqs = np.concatenate([freqs, freqs], -1)
    base = (half_idx + 25.6) / 89.6
    power = (t - N // 2) / 256.0
    scale = base[None, :] ** power[:, None]
    scale = np.concatenate([scale, scale], -1)
    return (np.cos(freqs) * scale, np.sin(freqs) * scale,
            np.cos(freqs) / scale, np.sin(freqs) / scale)


def _rot_half_mat():
    R = np.zeros((64, 64), np.float64)
    for d in range(32):
        R[d, d + 32] = -1.0
        R[d + 32, d] = 1.0
    return R


def host_constants():
    c = {}
    cosO, sinO, R_O = _outer_rope_tables()
    c["cosO"] = np.ascontiguousarray(cosO.T).astype(BF)
    c["sinO"] = np.ascontiguousarray(sinO.T).astype(BF)
    R_OT128 = np.zeros((128, 128), np.float64); R_OT128[:64, :64] = R_O.T
    c["R_OT"] = np.ascontiguousarray(R_OT128).astype(BF)
    cq, sq, ck, sk = _inner_xpos_tables()
    c["cq8"] = np.ascontiguousarray((8.0 * cq).T).astype(BF)
    c["sq8"] = np.ascontiguousarray((8.0 * sq).T).astype(BF)
    c["ckb"] = np.ascontiguousarray(ck.T).astype(BF)
    c["skb"] = np.ascontiguousarray(sk.T).astype(BF)
    R_IT128 = np.zeros((128, 128), np.float64); R_IT128[:64, :64] = _rot_half_mat().T
    c["R_IT"] = np.ascontiguousarray(R_IT128).astype(BF)
    pp = np.arange(128)[:, None]
    jj = np.arange(128)[None, :]
    c["Tsame"] = np.where(jj <= pp, 0.0, -1e30).astype(np.float32)
    c["Tprev"] = np.where(jj >= pp, 0.0, -1e30).astype(np.float32)
    c["Tboth"] = np.ascontiguousarray(np.concatenate(
        [c["Tprev"], c["Tsame"]], axis=1))
    c["idb"] = np.eye(128, dtype=BF)
    c["idf"] = np.eye(128, dtype=np.float32)
    c["ones64"] = np.ones((64, 1), BF)
    c["ones128"] = np.ones((128, 1), BF)
    c["ones_1x64"] = np.ones((1, 64), BF)
    c["ones_1x128"] = np.ones((1, 128), BF)
    c["onesf_1x128"] = np.ones((1, 128), np.float32)
    return c


def host_weights(w):
    o = {}
    def gt8(M):
        t = M.reshape(8, 128, 8, 128)
        return np.ascontiguousarray(t.transpose(2, 1, 0, 3).reshape(8, 128, 1024))
    o["wq_t"] = gt8(w["L0_attn_Wq"])
    o["wk_t"] = gt8(w["L0_attn_Wk"])
    o["wv_t"] = gt8(w["L0_attn_Wv"])
    o["wo_r"] = np.ascontiguousarray(w["L0_attn_Wo"].reshape(8, 128, 1024))
    o["bq_dm"] = np.ascontiguousarray(w["L0_attn_bq"].reshape(8, 128).T)
    o["bk_dm"] = np.ascontiguousarray(w["L0_attn_bk"].reshape(8, 128).T)
    o["bv_dm"] = np.ascontiguousarray(w["L0_attn_bv"].reshape(8, 128).T)
    o["bo_rep"] = np.ascontiguousarray(np.broadcast_to(w["L0_attn_bo"], (128, 1024)))

    wqkv = w["L0_ffn_wqkv"]
    t = wqkv[:, :, :1024].reshape(2, 8, 128, 8, 128)
    o["wqkv_qk"] = np.ascontiguousarray(t.transpose(0, 3, 2, 1, 4).reshape(2, 8, 128, 1024))
    o["wv_i"] = np.ascontiguousarray(wqkv[:, :, 1024:].reshape(2, 8, 128, 512))
    t = w["L0_ffn_wo"].reshape(2, 4, 128, 8, 128)
    o["wo_i"] = np.ascontiguousarray(t.transpose(0, 3, 2, 1, 4).reshape(2, 8, 128, 512))
    o["lng_dm"] = np.ascontiguousarray(w["L0_ffn_ln_g"].reshape(2, 8, 128).transpose(0, 2, 1))
    o["lnb_dm"] = np.ascontiguousarray(w["L0_ffn_ln_b"].reshape(2, 8, 128).transpose(0, 2, 1))

    w1 = w["L0_ffn_w1"]
    w1a = np.zeros((2, 1024, NU * 128), np.float32); w1a[:, :, :HID] = w1[:, :, :HID]
    w1g = np.zeros((2, 1024, NU * 128), np.float32); w1g[:, :, :HID] = w1[:, :, HID:]
    def ut(M):
        t = M.reshape(2, 8, 128, NU, 128)
        return np.ascontiguousarray(t.transpose(0, 3, 2, 1, 4).reshape(2, NU, 128, 1024))
    o["w1a_t"] = ut(w1a)
    o["w1g_t"] = ut(w1g)
    b1 = w["L0_ffn_b1"]
    b1a = np.zeros((2, NU * 128), np.float32); b1a[:, :HID] = b1[:, :HID]
    b1g = np.zeros((2, NU * 128), np.float32); b1g[:, :HID] = b1[:, HID:]
    o["b1a_dm"] = np.ascontiguousarray(b1a.reshape(2, NU, 128).transpose(0, 2, 1))
    o["b1g_dm"] = np.ascontiguousarray(b1g.reshape(2, NU, 128).transpose(0, 2, 1))
    w2p = np.zeros((2, NU * 128, 1024), np.float32)
    w2p[:, :HID, :] = w["L0_ffn_w2"]
    t = w2p.reshape(2, NU, 128, 8, 128)
    o["w2_t"] = np.ascontiguousarray(t.transpose(0, 3, 2, 1, 4).reshape(2, 8, 128, NU * 128))
    o["b2_dm"] = np.ascontiguousarray(w["L0_ffn_b2"].reshape(2, 8, 128).transpose(0, 2, 1))

    qs, ks = w["L0_ffn_q_scale"], w["L0_ffn_k_scale"]
    o["qs_c"] = np.ascontiguousarray(qs.reshape(2, 64, 1))
    o["qsr_c"] = np.ascontiguousarray(np.concatenate([qs[:, 32:], qs[:, :32]], 1).reshape(2, 64, 1))
    o["ks_c"] = np.ascontiguousarray(ks.reshape(2, 64, 1))
    o["ksr_c"] = np.ascontiguousarray(np.concatenate([ks[:, 32:], ks[:, :32]], 1).reshape(2, 64, 1))
    return o



ENGINES = ("tensor", "vector", "scalar", "gpsimd", "sync")

def _reg(ap):
    t = ap.tensor
    nm = t.name
    cls = type(t).__name__
    esz = DT_SZ.get(ap.dtype, 4)
    if cls == "DRamTensorHandle":
        return (nm, 0, 1 << 60)
    dims = list(ap.ap)
    if not dims:
        return (nm, 0, 1 << 60)
    pstep = dims[0][0]
    off = ap.offset % pstep if pstep > 0 else ap.offset
    reach = 1
    for stp, cnt in dims[1:]:
        reach += abs(stp) * (cnt - 1)
    lo, hi = off * esz, (off + reach) * esz
    if cls == "PSumTensorHandle":
        lo = (lo // 2048) * 2048
        hi = ((hi + 2047) // 2048) * 2048
    return (nm, lo, hi)


class Prog:
    def __init__(self):
        self.recs = []
        self.eng = {e: [] for e in ENGINES}
        self.wlog = {}
        self.rlog = {}

    def op(self, engine, fn, reads=(), writes=(), dma=None):
        oid = len(self.recs)
        deps = set()
        rregs = [_reg(a) for a in reads]
        wregs = [_reg(a) for a in writes]
        for nm, lo, hi in rregs:
            for l2, h2, o2 in self.wlog.get(nm, ()):
                if lo < h2 and l2 < hi:
                    deps.add(o2)
        for nm, lo, hi in wregs:
            for l2, h2, o2 in self.wlog.get(nm, ()):
                if lo < h2 and l2 < hi:
                    deps.add(o2)
            for l2, h2, o2 in self.rlog.get(nm, ()):
                if lo < h2 and l2 < hi:
                    deps.add(o2)
        rec = dict(id=oid, engine=engine, fn=fn, dma=dma, deps=deps, needs_inc=False)
        self.recs.append(rec)
        self.eng[engine].append(rec)
        for nm, lo, hi in rregs:
            self.rlog.setdefault(nm, []).append((lo, hi, oid))
        for nm, lo, hi in wregs:
            lst = self.wlog.setdefault(nm, [])
            lst[:] = [e for e in lst if not (lo <= e[0] and e[1] <= hi)]
            lst.append((lo, hi, oid))
            rl = self.rlog.get(nm)
            if rl:
                rl[:] = [e for e in rl if not (lo <= e[0] and e[1] <= hi)]
        return oid

    NDSEM = 24

    def finalize(self):
        dix = {"w": 0, "io": 0}
        for rec in self.recs:
            if rec["dma"] is not None:
                s = rec["dma"]
                rec["dslot"] = (s, dix[s] % self.NDSEM)
                dix[s] += 1
        for rec in self.recs:
            per = {}
            for d in rec["deps"]:
                p = self.recs[d]
                if (p["engine"] == "tensor" and rec["engine"] == "tensor"
                        and p["dma"] is None and rec["dma"] is None):
                    continue
                key = ("dma", p["dslot"]) if p["dma"] is not None else ("eng", p["engine"])
                if per.get(key, -1) < d:
                    per[key] = d
            rec["waits"] = per
            for d in per.values():
                self.recs[d]["needs_inc"] = True
        cnt = {e: 0 for e in ENGINES}
        dcnt = {}
        for rec in self.recs:
            if rec["dma"] is not None:
                k = rec["dslot"]
                dcnt[k] = dcnt.get(k, 0) + 16
                rec["done_val"] = dcnt[k]
                rec["needs_inc"] = True
                rec["self_wait"] = dcnt[k] - 16 if dcnt[k] > 16 else None
            elif rec["needs_inc"]:
                cnt[rec["engine"]] += 1
                rec["done_val"] = cnt[rec["engine"]]
            else:
                rec["done_val"] = None
        self.final_dma = dcnt

    def emit(self, engine, eh, sems, dsems):
        wm = {}
        for rec in self.eng[engine]:
            for key, d in sorted(rec["waits"].items(), key=lambda kv: kv[1]):
                v = self.recs[d]["done_val"]
                if wm.get(key, -1) >= v:
                    continue
                sem = dsems[key[1]] if key[0] == "dma" else sems[key[1]]
                eh.wait_ge(sem, v)
                wm[key] = v
            if rec["dma"] is not None and rec.get("self_wait"):
                k2 = ("dma", rec["dslot"])
                if wm.get(k2, -1) < rec["self_wait"]:
                    eh.wait_ge(dsems[rec["dslot"]], rec["self_wait"])
                    wm[k2] = rec["self_wait"]
            ins = rec["fn"](eh)
            if rec["dma"] is not None:
                ins.then_inc(dsems[rec["dslot"]], 16)
            elif rec["needs_inc"]:
                ins.then_inc(sems[rec["engine"]], 1)



def build_program(debug=False):
    nc = bass.Bass("TRN2", target_bir_lowering=False)
    P = Prog()

    x_d = nc.dram_tensor("x", [N, D], F32, kind="ExternalInput").ap()
    out_d = nc.dram_tensor("out", [N, D], F32, kind="ExternalOutput").ap()
    kscr = "ExternalOutput" if debug else "Internal"
    x1s_d = nc.dram_tensor("x1s", [N, D], F32, kind=kscr).ap()
    ctxs_d = nc.dram_tensor("ctxs", [D, N], BF16, kind=kscr).ap()

    consts = host_constants()
    dconst = {nm: nc.dram_tensor("c_" + nm, list(a.shape),
                                 BF16 if a.dtype == BF else F32,
                                 kind="ExternalInput").ap()
              for nm, a in consts.items()}
    wshapes = {
        "wq_t": (8, 128, 1024), "wk_t": (8, 128, 1024), "wv_t": (8, 128, 1024),
        "wo_r": (8, 128, 1024), "bq_dm": (128, 8), "bk_dm": (128, 8),
        "bv_dm": (128, 8), "bo_rep": (128, 1024),
        "wqkv_qk": (2, 8, 128, 1024), "wv_i": (2, 8, 128, 512),
        "wo_i": (2, 8, 128, 512), "lng_dm": (2, 128, 8), "lnb_dm": (2, 128, 8),
        "w1a_t": (2, NU, 128, 1024), "w1g_t": (2, NU, 128, 1024),
        "b1a_dm": (2, 128, NU), "b1g_dm": (2, 128, NU),
        "w2_t": (2, 8, 128, NU * 128), "b2_dm": (2, 128, 8),
        "qs_c": (2, 64, 1), "qsr_c": (2, 64, 1), "ks_c": (2, 64, 1), "ksr_c": (2, 64, 1),
    }
    dw = {nm: nc.dram_tensor(nm, list(sh), F32, kind="ExternalInput").ap()
          for nm, sh in wshapes.items()}

    dbg_out = {}
    def dbg_tensor(nm, shape, dt):
        ap = nc.dram_tensor("dbg_" + nm, list(shape), dt, kind="ExternalOutput").ap()
        dbg_out[nm] = ap
        return ap

    stack = contextlib.ExitStack()
    S = lambda name, shape, dt: stack.enter_context(nc.sbuf_tensor(name, shape, dt)).ap()
    PSU = lambda name, shape, dt: stack.enter_context(nc.psum_tensor(name, shape, dt)).ap()

    with stack:
        SCO_A = PSU("SCO_A", [128, 1024], F32)
        SCO_B = PSU("SCO_B", [128, 1024], F32)
        PSX = PSU("PSX", [128, 1024], F32)
        TRP = PSX.bitcast(BF16)
        CTX = PSU("CTX", [128, 512], F32)
        AUX = PSU("AUX", [128, 512], F32)

        xfT = [S(f"xfT{c}", [128, N + 1], F32) for c in range(DCH)]
        cq8 = S("cq8", [64, N], BF16); sq8 = S("sq8", [64, N], BF16)
        ckb = S("ckb", [64, N], BF16); skb = S("skb", [64, N], BF16)
        Tboth = S("Tboth", [128, 256], F32)
        Tprev = Tboth[:, 0:128]
        Tsame = Tboth[:, 128:256]
        idb = S("idb", [128, 128], BF16); idf = S("idf", [128, 128], F32)
        R_OT = S("R_OT", [128, 128], BF16); R_IT = S("R_IT", [128, 128], BF16)
        ones64 = S("ones64", [64, 1], BF16)
        ones128 = S("ones128", [128, 1], BF16)
        ones_1x64 = S("o1x64", [1, 64], BF16)
        ones_1x128 = S("o1x128", [1, 128], BF16)
        onesf_1x128 = S("of1x128", [1, 128], F32)
        bq_sb = S("bq_sb", [128, 8], F32); bk_sb = S("bk_sb", [128, 8], F32)
        bv_sb = S("bv_sb", [128, 8], F32)
        b1a_sb = S("b1a_sb", [128, NU], F32); b1g_sb = S("b1g_sb", [128, NU], F32)
        b2_sb = S("b2_sb", [128, 8], F32)
        lng_sb = S("lng_sb", [128, 8], F32); lnb_sb = S("lnb_sb", [128, 8], F32)
        qs_sb = S("qs_sb", [64, 1], F32); qsr_sb = S("qsr_sb", [64, 1], F32)
        ks_sb = S("ks_sb", [64, 1], F32); ksr_sb = S("ksr_sb", [64, 1], F32)

        hT = [S(f"hT{c}", [128, N], BF16) for c in range(DCH)]
        Bp = [S(f"Bp{i}", [128, N], BF16) for i in range(4)]
        QKt = [S(f"QK{i}", [128, N], BF16) for i in range(4)]
        QK = [q[0:64, :] for q in QKt]
        stg = [S(f"stg{c}", [128, 513], BF16) for c in range(DCH)]
        va = [S(f"va{i}", [128, 16 * 65 + 64], BF16) for i in range(2)]
        E_A = S("E_A", [128, 1024], BF16); E_B = S("E_B", [128, 1024], BF16)
        sq_sc = S("sq_sc", [64, N], BF16)
        wqb = [S(f"wqb{i}", [128, 1024], BF16) for i in range(2)]
        wkb = [S(f"wkb{i}", [128, 1024], BF16) for i in range(2)]
        wvb = [S(f"wvb{i}", [128, 1024], BF16) for i in range(2)]
        w2b = [S(f"w2b{i}", [128, 11 * 128], BF16) for i in range(2)]
        h0_sb = S("h0_sb", [128, 1024], F32)
        ag_sb = S("ag_sb", [128, 512], F32)
        TMP = S("TMP", [128, 512], F32)
        st_sc = S("st_sc", [128, 16], F32)
        sv13 = S("sv13", [128, 8], F32)
        red_sc = S("red_sc", [1, 8], F32)
        negB = [S(f"negB{i}", [128, 1], F32) for i in range(2)]
        rn_bf = S("rn_bf", [1, 512], BF16)
        rd_bf = rn_bf
        nb_bf = rn_bf[0:1, 0:1]
        mu_s = S("mu_s", [1, 512], F32); rs_s = S("rs_s", [1, 512], F32)
        den_i = [S(f"den_i{j}", [128, 1], F32) for j in range(2)]
        rden = [S(f"rden{j}", [128, 1], F32) for j in range(2)]
        negm = [S(f"negm{j}", [128, 1], F32) for j in range(2)]
        trow = S("trow", [1, 512], F32)
        rdb_sb = S("rdb", [64, 512], F32)
        tmp_a = rdb_sb
        tmp_b = S("tmpb", [64, 512], F32)
        cbo = [S(f"cbo{i}", [64, 512], BF16) for i in range(2)]
        cosO = Bp[3][0:64, :]
        sinO = Bp[2][0:64, :]
        bo_rep = Bp[2].bitcast(F32)
        gg_sb = TMP.bitcast(BF16)[:, 0:512]
        ag_bf = TMP.bitcast(BF16)[:, 512:1024]
        E_rawb = TMP.bitcast(BF16)[:, 0:640]
        woib = [wvb[i][:, 0:512] for i in range(2)]

        wo_sb = [hT[c // 2][:, (c % 2) * 1024:(c % 2 + 1) * 1024] for c in range(8)]
        v_i = [hT[c // 4][:, (c % 4) * 512:(c % 4 + 1) * 512] for c in range(16)]
        ctxT_i = [hT[4 + i] for i in range(4)]
        wvi = [hT[4 + c // 4][:, (c % 4) * 512:(c % 4 + 1) * 512] for c in range(8)]
        x0_sb = [hT[4].bitcast(F32), hT[5].bitcast(F32)]
        x1w_sb = [hT[6].bitcast(F32), hT[7].bitcast(F32)]
        h2T = ([hT[c // 2][:, (c % 2) * 1024:(c % 2 + 1) * 1024] for c in range(16)]
               + [va[0][:, 0:1024], va[1][:, 0:1024], E_A[:, :], E_B[:, :],
                  QKt[0][:, 0:1024], QKt[1][:, 0:1024]])
        x1r_sb = [hT[0].bitcast(F32), hT[1].bitcast(F32)]
        outr_sb = [hT[2].bitcast(F32), hT[3].bitcast(F32)]
        xin_sb = [Bp[1].bitcast(F32), Bp[2].bitcast(F32)]
        h_sb = [Bp[0][:, 0:1024], Bp[0][:, 1024:2048]]
        vT_sb = Bp[1]
        csl = ([Bp[0][:, i * 512:(i + 1) * 512] for i in range(4)]
               + [Bp[1][:, i * 512:(i + 1) * 512] for i in range(4)])
        sqb_i = Bp[2][:, 0:1024]
        En_i = [Bp[2][:, 1024:1664], Bp[3][:, 0:640]]
        ET_i = [Bp[3][:, 640:1280], Bp[3][:, 1280:1920]]
        _hnT_homes = [Bp[0], Bp[1], Bp[3], QKt[2]]
        hnT = [_hnT_homes[c // 2][:, (c % 2) * 1024:(c % 2 + 1) * 1024] for c in range(8)]
        rnb = [va[0].bitcast(F32)[0:64, 0:512], va[1].bitcast(F32)[0:64, 0:512]]

        sems = {e: stack.enter_context(nc.semaphore(f"sem_{e}")) for e in ENGINES}
        dsems = {}
        for k in ("w", "io"):
            for j in range(Prog.NDSEM):
                dsems[(k, j)] = stack.enter_context(nc.semaphore(f"dma_{k}{j}"))

        def dma(dst, src, stream, eng=None):
            eng = eng or ("gpsimd" if stream == "w" else "sync")
            P.op(eng, lambda e, d=dst, s=src: e.dma_start(d, s),
                 reads=[src], writes=[dst], dma=stream)

        def mm(out, lhsT, rhs, start, stop):
            P.op("tensor", lambda e, o=out, l=lhsT, r=rhs, a=start, b=stop:
                 e.matmul(o, l, r, start=a, stop=b), reads=[lhsT, rhs], writes=[out])

        def transp(out, in_, ident):
            P.op("tensor", lambda e, o=out, i=in_, d=ident: e.transpose(o, i, d),
                 reads=[in_, ident], writes=[out])

        def tt(out, a, b, op, eng="vector"):
            P.op(eng, lambda e, o=out, x=a, y=b, p=op: e.tensor_tensor(o, x, y, p),
                 reads=[a, b], writes=[out])

        def ts(out, in0, s1, s2, op0, op1=ALU.bypass):
            rd = [in0] + [s for s in (s1, s2) if hasattr(s, "tensor")]
            P.op("vector", lambda e, o=out, i=in0, a=s1, b=s2, p0=op0, p1=op1:
                 e.tensor_scalar(o, i, a, b, p0, p1), reads=rd, writes=[out])

        def stt(out, in0, s, in1, op0, op1):
            rd = [in0, in1] + ([s] if hasattr(s, "tensor") else [])
            P.op("vector", lambda e, o=out, a=in0, c=s, b=in1, p0=op0, p1=op1:
                 e.scalar_tensor_tensor(o, a, c, b, p0, p1), reads=rd, writes=[out])

        def act(out, in_, func, bias=0.0, scale=1.0, accum=None):
            rd = [in_] + [b for b in (bias, scale) if hasattr(b, "tensor")]
            wr = [out] + ([accum] if accum is not None else [])
            P.op("scalar", lambda e, o=out, i=in_, f=func, b=bias, s=scale, a=accum:
                 e.activation(o, i, f, bias=b, scale=s, accum_out=a), reads=rd, writes=wr)

        def red(out, in_, op, negate=False):
            P.op("vector", lambda e, o=out, i=in_, p=op, n=negate:
                 e.tensor_reduce(o, i, AX.X, p, negate=n), reads=[in_], writes=[out])

        def cp(out, in_, eng="vector"):
            if eng == "scalar":
                P.op(eng, lambda e, o=out, i=in_: e.copy(o, i), reads=[in_], writes=[out])
            else:
                P.op(eng, lambda e, o=out, i=in_: e.tensor_copy(o, i),
                     reads=[in_], writes=[out])

        def memset(ap, v, eng="gpsimd"):
            P.op(eng, lambda e, a=ap, c=v: e.memset(a, c), reads=[], writes=[ap])

        def recip(out, in_):
            P.op("vector", lambda e, o=out, i=in_: e.reciprocal(o, i),
                 reads=[in_], writes=[out])

        for nm, sb in (("cosO", cosO), ("R_OT", R_OT), ("R_IT", R_IT),
                       ("cq8", cq8), ("sq8", sq8), ("ckb", ckb), ("skb", skb),
                       ("Tboth", Tboth), ("idb", idb), ("idf", idf),
                       ("ones64", ones64), ("ones128", ones128),
                       ("ones_1x64", ones_1x64), ("ones_1x128", ones_1x128),
                       ("onesf_1x128", onesf_1x128)):
            dma(sb, dconst[nm], "io")
        dma(bq_sb, dw["bq_dm"], "io"); dma(bk_sb, dw["bk_dm"], "io")
        dma(bv_sb, dw["bv_dm"], "io")
        for c in range(DCH):
            memset(xfT[c][:, 0:1], 0.0)
        for i in range(2):
            memset(va[i][:, 64::65], 1.0, eng="vector")
            memset(va[i][:, 1040:1104], 0.0, eng="vector")
        for i in range(4):
            memset(QKt[i][64:128, :], 0.0, eng="vector")

        def ln_rowstats(x_in, width, rb=0):
            c = lambda j: st_sc[:, rb + j:rb + j + 1]
            red(c(4), x_in, ALU.add)
            act(SCO_B[:, 0:width], x_in, AF.Square, accum=c(5))
            ts(c(0), c(4), 1.0 / width, None, ALU.mult)
            tt(c(1), c(0), c(0), ALU.mult)
            stt(c(2), c(5), 1.0 / width, c(1), ALU.mult, ALU.subtract)
            ts(c(7), c(2), 1e-5, None, ALU.add)
            act(c(6), c(7), AF.Ln)
            act(c(3), c(6), AF.Exp, scale=-0.5)

        for t in range(NCH):
            xb = xin_sb[t % 2]
            dma(xb, x_d[t * 128:(t + 1) * 128, :], "io")
            rb = 8 * (t % 2)
            ln_rowstats(xb, D, rb)
            hb = h_sb[t % 2]
            ts(hb, xb, st_sc[:, rb:rb + 1], st_sc[:, rb + 3:rb + 4],
               ALU.subtract, ALU.mult)
            for c in range(DCH):
                slot = TRP[:, c * 256:c * 256 + 128]
                transp(slot, hb[:, c * 128:(c + 1) * 128], idb)
                cp(hT[c][:, t * 128:(t + 1) * 128], slot, eng="scalar")

        if debug:
            dma(dbg_tensor("hT0", (128, N), BF16), hT[0][:, :], "io")
        dma(sinO, dconst["sinO"], "io")

        for g in range(2):
            dma(wqb[g], dw["wq_t"][g], "w")
            dma(wkb[g], dw["wk_t"][g], "w")
            dma(wvb[g], dw["wv_t"][g], "w")
        qa, qb_, ka, kb_ = QK

        for g in range(DCH):
            wq, wk, wv = wqb[g % 2], wkb[g % 2], wvb[g % 2]
            bq0, bq1 = bq_sb[0:64, g:g + 1], bq_sb[64:128, g:g + 1]
            bk0, bk1 = bk_sb[0:64, g:g + 1], bk_sb[64:128, g:g + 1]
            for s in range(4):
                sl = slice(s * 512, (s + 1) * 512)
                q_ps = SCO_A[:, (s % 2) * 512:(s % 2) * 512 + 512]
                k_ps = SCO_B[:, (s % 2) * 512:(s % 2) * 512 + 512]
                for c in range(DCH):
                    mm(q_ps, wq[:, c * 128:(c + 1) * 128], hT[c][:, sl], c == 0, c == 7)
                for c in range(DCH):
                    mm(k_ps, wk[:, c * 128:(c + 1) * 128], hT[c][:, sl], c == 0, c == 7)
                for c in range(DCH):
                    mm(CTX, wv[:, c * 128:(c + 1) * 128], hT[c][:, sl], c == 0, c == 7)
                act(qa[:, sl], q_ps[0:64, :], AF.Identity, bias=bq0)
                act(qb_[:, sl], q_ps[64:128, :], AF.Identity, bias=bq1)
                act(ka[:, sl], k_ps[0:64, :], AF.Identity, bias=bk0)
                act(kb_[:, sl], k_ps[64:128, :], AF.Identity, bias=bk1)
                act(vT_sb[:, sl], CTX, AF.Identity, bias=bv_sb[:, g:g + 1])
            if debug and g == 0:
                dma(dbg_tensor("qn0", (64, N), BF16), qa[:, :], "io")
                dma(dbg_tensor("kn0", (64, N), BF16), ka[:, :], "io")
                dma(dbg_tensor("vT0", (128, N), BF16), vT_sb[:, :], "io")
            for t in range(NCH):
                slot = TRP[:, (t % 8) * 256:(t % 8) * 256 + 128]
                transp(slot, vT_sb[:, t * 128:(t + 1) * 128], idb)
                cp(va[0][:, t * 65:t * 65 + 64], slot[:, 0:64])
                cp(va[1][:, t * 65:t * 65 + 64], slot[:, 64:128])
            if g == 7:
                for c in range(DCH):
                    dma(wo_sb[c], dw["wo_r"][c], "w")
            for i in range(2):
                qv = (qa, qb_)[i]
                kv = (ka, kb_)[i]
                qvp = (QKt[0], QKt[1])[i]
                kvp = (QKt[2], QKt[3])[i]
                for s in range(4):
                    sl = slice(s * 512, (s + 1) * 512)
                    mm(AUX[:, :], R_OT, qvp[:, sl], True, True)
                    tt(tmp_a, qv[:, sl], cosO[:, sl], ALU.mult)
                    tt(tmp_b, AUX[0:64, :], sinO[:, sl], ALU.mult)
                    tt(qv[:, sl], tmp_a, tmp_b, ALU.add)
                    mm(AUX[:, :], R_OT, kvp[:, sl], True, True)
                    tt(tmp_a, kv[:, sl], cosO[:, sl], ALU.mult)
                    tt(tmp_b, AUX[0:64, :], sinO[:, sl], ALU.mult)
                    tt(kv[:, sl], tmp_a, tmp_b, ALU.add)
                tt(sq_sc, qv, qv, ALU.mult)
                for s in range(4):
                    pb = (SCO_A, SCO_B)[s % 2][0:1, (s // 2) * 512:(s // 2) * 512 + 512]
                    mm(pb, ones64, sq_sc[:, s * 512:(s + 1) * 512], True, True)
                    red(red_sc[:, s:s + 1], pb, ALU.max)
                red(red_sc[:, 4:5], red_sc[:, 0:4], ALU.max)
                tt(sq_sc, kv, kv, ALU.mult)
                for s in range(4):
                    pb = (SCO_A, SCO_B)[s % 2][0:1, (s // 2) * 512:(s // 2) * 512 + 512]
                    mm(pb, ones64, sq_sc[:, s * 512:(s + 1) * 512], True, True)
                    red(red_sc[:, s:s + 1], pb, ALU.max)
                red(red_sc[:, 5:6], red_sc[:, 0:4], ALU.max)
                tt(red_sc[:, 6:7], red_sc[:, 4:5], red_sc[:, 5:6], ALU.mult)
                act(red_sc[:, 7:8], red_sc[:, 6:7], AF.Ln)
                act(red_sc[:, 6:7], red_sc[:, 7:8], AF.Exp, scale=0.5)
                ts(nb_bf, red_sc[:, 6:7], -0.125, None, ALU.mult)
                mm(AUX[:, 0:1], ones_1x128, nb_bf, True, True)
                cp(negB[i], AUX[:, 0:1], eng="scalar")
                if debug and 2 * g + i == 0:
                    dma(dbg_tensor("qf0", (64, N), BF16), qv[:, :], "io")
                    dma(dbg_tensor("kf0", (64, N), BF16), kv[:, :], "io")
                    dma(dbg_tensor("negB0", (128, 1), F32), negB[i][:, :], "io")
                    dma(dbg_tensor("va0", (128, 1104), BF16), va[0][:, :], "io")
            QVp = (QKt[0], QKt[1]); KVp = (QKt[2], QKt[3])
            SCs = (SCO_A, SCO_B)
            Es = ((E_A, E_B), (h0_sb.bitcast(BF16)[:, 0:1024],
                               h0_sb.bitcast(BF16)[:, 1024:2048]))
            CTXs = (CTX, PSX[:, 0:512])
            for s in range(4):
                qsl = slice(s * 512, (s + 1) * 512)
                def emit_ctx2(i, b):
                    EE = Es[i][b % 2]
                    for j in range(2):
                        kc = 2 * b + j
                        mm(CTXs[i], va[i][:, kc * 65:kc * 65 + 128],
                           EE[:, j * 512:(j + 1) * 512], kc == 0, kc == 15)
                for b in range(8):
                    for i in range(2):
                        for j in range(2):
                            kc = 2 * b + j
                            mm(SCs[i][:, j * 512:(j + 1) * 512],
                               KVp[i][:, kc * 128:(kc + 1) * 128],
                               QVp[i][:, qsl], True, True)
                    for i in range(2):
                        act(Es[i][b % 2], SCs[i], AF.Exp, bias=negB[i], scale=0.125)
                    if b > 0:
                        emit_ctx2(0, b - 1)
                        emit_ctx2(1, b - 1)
                emit_ctx2(0, 7)
                emit_ctx2(1, 7)
                for i in range(2):
                    h = 2 * g + i
                    act(trow, CTXs[i][64:65, :], AF.Ln)
                    act(rd_bf, trow, AF.Exp, scale=-1.0)
                    mm(PSX[0:64, 512:1024], ones_1x64, rd_bf, True, True)
                    cp(rdb_sb, PSX[0:64, 512:1024], eng="scalar")
                    tt(cbo[i], CTXs[i][0:64, :], rdb_sb, ALU.mult)
                    dma(ctxs_d[h * 64:(h + 1) * 64, s * 512:(s + 1) * 512],
                        cbo[i], "io")

            if g + 2 < DCH:
                dma(wqb[g % 2], dw["wq_t"][g + 2], "w")
                dma(wkb[g % 2], dw["wk_t"][g + 2], "w")
                dma(wvb[g % 2], dw["wv_t"][g + 2], "w")

        dma(bo_rep, dw["bo_rep"], "io")
        for tb in range(4):
            for c in range(DCH):
                dma(csl[c], ctxs_d[c * 128:(c + 1) * 128, tb * 512:(tb + 1) * 512], "io")
            for tq in range(4):
                t = tb * 4 + tq
                xb = x0_sb[tq % 2]
                dma(xb, x_d[t * 128:(t + 1) * 128, :], "io")
                x1b = x1w_sb[tq % 2]
                for ns in range(2):
                    ps = (SCO_A, SCO_B)[ns][:, 0:512]
                    for c in range(DCH):
                        mm(ps, csl[c][:, tq * 128:(tq + 1) * 128],
                           wo_sb[c][:, ns * 512:(ns + 1) * 512], c == 0, c == 7)
                    sl = slice(ns * 512, (ns + 1) * 512)
                    tt(x1b[:, sl], ps, bo_rep[:, sl], ALU.add)
                tt(x1b, x1b, xb, ALU.add)
                dma(x1s_d[t * 128:(t + 1) * 128, :], x1b, "io")
                rb = 8 * (t % 2)
                ln_rowstats(x1b, D, rb)
                ts(h0_sb, x1b, st_sc[:, rb:rb + 1], st_sc[:, rb + 3:rb + 4],
                   ALU.subtract, ALU.mult)
                for c in range(DCH):
                    slot = SCO_A[:, c * 128:(c + 1) * 128]
                    transp(slot, h0_sb[:, c * 128:(c + 1) * 128], idf)
                    cp(xfT[c][:, 1 + t * 128:1 + (t + 1) * 128], slot)

        def restage(sl):
            for c in range(DCH):
                cp(stg[c], xfT[c][:, sl * 512:sl * 512 + 513], eng="gpsimd")

        def stg_sh(c, lo, width):
            off = lo if c >= 4 else lo + 1
            return stg[c][:, off:off + width]

        def xf_sh(c, sl, width=512):
            off = sl * 512 if c >= 4 else sl * 512 + 1
            return xfT[c][:, off:off + width]

        for l in range(2):
            dma(qs_sb, dw["qs_c"][l], "io"); dma(qsr_sb, dw["qsr_c"][l], "io")
            dma(ks_sb, dw["ks_c"][l], "io"); dma(ksr_sb, dw["ksr_c"][l], "io")
            dma(b1a_sb, dw["b1a_dm"][l], "io"); dma(b1g_sb, dw["b1g_dm"][l], "io")
            dma(b2_sb, dw["b2_dm"][l], "io")
            dma(lng_sb, dw["lng_dm"][l], "io"); dma(lnb_sb, dw["lnb_dm"][l], "io")
            for c in range(DCH):
                dma(wvi[c], dw["wv_i"][l][c], "w")
            for sl in range(4):
                restage(sl)
                for j in range(4):
                    tch = sl * 4 + j
                    for c in range(DCH):
                        mm(CTX, stg_sh(c, j * 128, 128), wvi[c], c == 0, c == 7)
                    cp(v_i[tch], CTX)
            for i4 in range(4):
                memset(QKt[i4][64:128, :], 0.0, eng="vector")
            for p in range(2):
                dma(wqb[p], dw["wqkv_qk"][l][p], "w")
                dma(wkb[p], dw["wqkv_qk"][l][4 + p], "w")
            chain_banks = [CTX, AUX, PSX[:, 0:512], PSX[:, 512:1024]]
            sq_slots = [Bp[2][0:64, j * 512:(j + 1) * 512] for j in range(4)]
            mu_slots = [mu_s, rs_s, trow, h0_sb[0:1, 0:512]]
            rn_slots = [rn_bf, cbo[0][0:1, 0:512], cbo[1][0:1, 0:512],
                        TMP.bitcast(BF16)[0:1, 0:512]]
            rnb_slots = [rnb[0], rnb[1], sq_sc.bitcast(F32)[:, 0:512],
                         sq_sc.bitcast(F32)[:, 512:1024]]

            def proj_emit(p_, sl_):
                wqp, wkp = wqb[p_ % 2], wkb[p_ % 2]
                q_ps = SCO_A[:, (sl_ % 2) * 512:(sl_ % 2) * 512 + 512]
                k_ps = SCO_B[:, (sl_ % 2) * 512:(sl_ % 2) * 512 + 512]
                for c in range(DCH):
                    mm(q_ps, wqp[:, c * 128:(c + 1) * 128], stg_sh(c, 0, 512),
                       c == 0, c == 7)
                for c in range(DCH):
                    mm(k_ps, wkp[:, c * 128:(c + 1) * 128], stg_sh(c, 0, 512),
                       c == 0, c == 7)
                if sl_ + 1 < 4:
                    restage(sl_ + 1)

            for p in range(4):
                restage(0)
                proj_emit(p, 0)
                for sl in range(4):
                    slc = slice(sl * 512, (sl + 1) * 512)
                    q_ps = SCO_A[:, (sl % 2) * 512:(sl % 2) * 512 + 512]
                    k_ps = SCO_B[:, (sl % 2) * 512:(sl % 2) * 512 + 512]
                    chains = []
                    for i in range(2):
                        rows = slice(i * 64, (i + 1) * 64)
                        for wq_, (ps_, outv, outvp, csT, snT, sc_, scr_) in (
                                (0, (q_ps, (qa, qb_)[i], (QKt[0], QKt[1])[i],
                                     cq8, sq8, qs_sb, qsr_sb)),
                                (1, (k_ps, (ka, kb_)[i], (QKt[2], QKt[3])[i],
                                     ckb, skb, ks_sb, ksr_sb))):
                            chains.append(dict(rows=rows, ps=ps_, outv=outv,
                                               outvp=outvp, csT=csT, snT=snT,
                                               sc=sc_, scr=scr_,
                                               bank=chain_banks[2 * wq_ + i],
                                               sq=sq_slots[2 * wq_ + i],
                                               mu=mu_slots[2 * wq_ + i],
                                               rn=rn_slots[2 * wq_ + i],
                                               rnb=rnb_slots[2 * wq_ + i]))
                    for ch in chains:
                        act(ch["sq"], ch["ps"][ch["rows"], :], AF.Square)
                    for ch in chains:
                        mm(ch["bank"][0:1, :], ones64, ch["sq"], True, True)
                    for ch in chains:
                        act(ch["mu"], ch["bank"][0:1, :], AF.Ln)
                    for ch in chains:
                        act(ch["rn"], ch["mu"], AF.Exp, scale=-0.5)
                    if sl + 1 < 4:
                        proj_emit(p, sl + 1)
                    for ch in chains:
                        mm(ch["bank"][0:64, :], ones_1x64, ch["rn"], True, True)
                    for ch in chains:
                        cp(ch["rnb"][0:64, :], ch["bank"][0:64, :], eng="scalar")
                    for ch in chains:
                        tt(ch["outv"][:, slc], ch["ps"][ch["rows"], :],
                           ch["rnb"][0:64, :], ALU.mult)
                    for ch in chains:
                        mm(ch["bank"][:, :] if ch["bank"].partition_size() == 128
                           else ch["bank"], R_IT, ch["outvp"][:, slc], True, True)
                    for ci, ch in enumerate(chains):
                        ta = (tmp_a, tmp_b)[ci % 2]
                        tb = [rnb_slots[2][0:64, :], rnb_slots[3][0:64, :],
                              rnb[0][0:64, :], rnb[1][0:64, :]][ci]
                        stt(ta, ch["outv"][:, slc], ch["sc"], ch["csT"][:, slc],
                            ALU.mult, ALU.mult)
                        stt(tb, ch["bank"][0:64, :], ch["scr"], ch["snT"][:, slc],
                            ALU.mult, ALU.mult)
                        tt(ch["outv"][:, slc], ta, tb, ALU.add, eng="gpsimd")
                E_raw2 = [E_rawb, h0_sb.bitcast(BF16)[:, 0:640]]

                def emit_tail(i, qc, z):
                    h = 2 * p + i
                    lo = max(0, qc - 4)
                    nb = qc + 1 - lo
                    width = nb * 128
                    for kk in range(nb):
                        slot = TRP[:, (z % 2) * 1024 + kk * 128:
                                   (z % 2) * 1024 + (kk + 1) * 128]
                        transp(slot, En_i[z % 2][:, kk * 128:(kk + 1) * 128], idb)
                    cp(ET_i[z % 2][:, 0:width],
                       TRP[:, (z % 2) * 1024:(z % 2) * 1024 + width],
                       eng=("scalar", "vector")[z % 2])
                    for kk in range(nb):
                        mm(CTX[:, (z % 4) * 128:(z % 4 + 1) * 128],
                           v_i[lo + kk][:, (h // 2) * 128:(h // 2) * 128 + 128],
                           ET_i[z % 2][:, kk * 128:(kk + 1) * 128],
                           kk == 0, kk == nb - 1)
                    cp(ctxT_i[p][i * 64:(i + 1) * 64, qc * 128:(qc + 1) * 128],
                       CTX[(h % 2) * 64:(h % 2) * 64 + 64,
                           (z % 4) * 128:(z % 4 + 1) * 128],
                       eng=("vector", "scalar")[z % 2])

                pend = []
                zs = [(qc, i) for qc in range(NCH) for i in range(2)]
                for z0 in range(0, len(zs), 2):
                    grp = []
                    for dz in range(2):
                        z = z0 + dz
                        qc, i = zs[z]
                        lo = max(0, qc - 4)
                        grp.append(dict(z=z, qc=qc, i=i, lo=lo,
                                        nb=qc + 1 - lo, base=lo * 128,
                                        width=(qc + 1 - lo) * 128,
                                        SC=(SCO_A, SCO_B)[z % 2],
                                        qvp=(QKt[0], QKt[1])[i],
                                        kvp=(QKt[2], QKt[3])[i]))
                    for g_ in grp:
                        w1_ = min(g_["width"], 512)
                        mm(g_["SC"][:, 0:w1_],
                           g_["qvp"][:, g_["qc"] * 128:(g_["qc"] + 1) * 128],
                           g_["kvp"][:, g_["base"]:g_["base"] + w1_], True, True)
                        if g_["width"] > 512:
                            mm(g_["SC"][:, 512:g_["width"]],
                               g_["qvp"][:, g_["qc"] * 128:(g_["qc"] + 1) * 128],
                               g_["kvp"][:, g_["base"] + 512:g_["base"] + g_["width"]],
                               True, True)
                    for g_ in grp:
                        if g_["qc"] >= 4:
                            scv = g_["SC"][:, 0:g_["width"]].rearrange(
                                "p (a b) -> p a b", b=128)[:, 0::4, :]
                            tt(scv, scv, Tboth.rearrange("p (a b) -> p a b", b=128),
                               ALU.add)
                        else:
                            tt(g_["SC"][:, g_["width"] - 128:g_["width"]],
                               g_["SC"][:, g_["width"] - 128:g_["width"]],
                               Tsame, ALU.add)
                    while pend:
                        emit_tail(*pend.pop(0))
                    for g_ in grp:
                        red(negm[g_["z"] % 2], g_["SC"][:, 0:g_["width"]],
                            ALU.max, negate=True)
                    for g_ in grp:
                        act(E_raw2[g_["z"] % 2][:, 0:g_["width"]],
                            g_["SC"][:, 0:g_["width"]], AF.Exp,
                            bias=negm[g_["z"] % 2], accum=den_i[g_["z"] % 2])
                    for g_ in grp:
                        act(negm[g_["z"] % 2], den_i[g_["z"] % 2], AF.Ln)
                    for g_ in grp:
                        act(rden[g_["z"] % 2], negm[g_["z"] % 2], AF.Exp, scale=-1.0)
                    for g_ in grp:
                        ts(En_i[g_["z"] % 2][:, 0:g_["width"]],
                           E_raw2[g_["z"] % 2][:, 0:g_["width"]],
                           rden[g_["z"] % 2], None, ALU.mult)
                    for g_ in grp:
                        pend.append((g_["i"], g_["qc"], g_["z"]))
                while pend:
                    emit_tail(*pend.pop(0))
                if p + 2 < 4:
                    dma(wqb[p % 2], dw["wqkv_qk"][l][p + 2], "w")
                    dma(wkb[p % 2], dw["wqkv_qk"][l][4 + p + 2], "w")
            for gq in range(2):
                dma(woib[gq], dw["wo_i"][l][gq], "w")
            for gq in range(DCH):
                for sl in range(4):
                    ps = (SCO_A, SCO_B)[sl % 2][:, 0:512]
                    for c in range(4):
                        mm(ps, woib[gq % 2][:, c * 128:(c + 1) * 128],
                           ctxT_i[c][:, sl * 512:(sl + 1) * 512], c == 0, c == 3)
                    dst = xfT[gq][:, 1 + sl * 512:1 + (sl + 1) * 512]
                    tt(dst, dst, ps, ALU.add)
                if gq + 2 < DCH:
                    dma(woib[gq % 2], dw["wo_i"][l][gq + 2], "w")
            for c in range(DCH):
                cp(sv13[:, c:c + 1], xfT[c][:, 1024:1025])
            for ms in range(2):
                for sl in (2 * ms, 2 * ms + 1):
                    restage(sl)
                    if sl == 2:
                        for c in range(4, DCH):
                            cp(stg[c][:, 0:1], sv13[:, c:c + 1])
                    for c in range(DCH):
                        mm(SCO_A[0:1, 0:512], ones128, stg_sh(c, 0, 512), c == 0, c == 7)
                    for c in range(DCH):
                        act(sqb_i[:, 0:512], stg_sh(c, 0, 512), AF.Square)
                        mm(SCO_B[0:1, 0:512], ones128, sqb_i[:, 0:512], c == 0, c == 7)
                    ts(mu_s, SCO_A[0:1, 0:512], 1.0 / D, None, ALU.mult)
                    tt(rs_s, mu_s, mu_s, ALU.mult)
                    stt(trow, SCO_B[0:1, 0:512], 1.0 / D, rs_s, ALU.mult, ALU.subtract)
                    ts(rs_s, trow, 1e-5, None, ALU.add)
                    act(trow, rs_s, AF.Ln)
                    act(rs_s, trow, AF.Exp, scale=-0.5)
                    mm(AUX[:, :], onesf_1x128, mu_s, True, True)
                    cp(ag_sb, AUX, eng="scalar")
                    mm(AUX[:, :], onesf_1x128, rs_s, True, True)
                    cp(h0_sb[:, 512:1024], AUX, eng="scalar")
                    for c in range(DCH):
                        tt(TMP, xf_sh(c, sl), ag_sb, ALU.subtract)
                        tt(h0_sb[:, 0:512], TMP, h0_sb[:, 512:1024], ALU.mult)
                        ts(hnT[c][:, (sl % 2) * 512:(sl % 2 + 1) * 512], h0_sb[:, 0:512],
                           lng_sb[:, c:c + 1], lnb_sb[:, c:c + 1], ALU.mult, ALU.add)
                    if sl == 2:
                        for c in range(4, DCH):
                            tt(TMP[:, 0:1], sv13[:, c:c + 1], ag_sb[:, 0:1], ALU.subtract)
                            tt(h0_sb[:, 0:1], TMP[:, 0:1], h0_sb[:, 512:513], ALU.mult)
                            ts(hnT[c][:, 0:1], h0_sb[:, 0:1],
                               lng_sb[:, c:c + 1], lnb_sb[:, c:c + 1], ALU.mult, ALU.add)
                dma(wkb[0], dw["w1a_t"][l][0], "w")
                dma(wvb[0], dw["w1g_t"][l][0], "w")
                for u in range(NU):
                    if u + 1 < NU:
                        dma(wkb[(u + 1) % 2], dw["w1a_t"][l][u + 1], "w")
                        dma(wvb[(u + 1) % 2], dw["w1g_t"][l][u + 1], "w")
                    for hf in range(2):
                        lt = slice(hf * 512, (hf + 1) * 512)
                        pa = SCO_A[:, hf * 512:(hf + 1) * 512]
                        pg = SCO_B[:, hf * 512:(hf + 1) * 512]
                        for c in range(DCH):
                            mm(pa, wkb[u % 2][:, c * 128:(c + 1) * 128],
                               hnT[c][:, lt], c == 0, c == 7)
                        for c in range(DCH):
                            mm(pg, wvb[u % 2][:, c * 128:(c + 1) * 128],
                               hnT[c][:, lt], c == 0, c == 7)
                        ts(ag_bf, pa, b1a_sb[:, u:u + 1], None, ALU.add)
                        act(gg_sb, pg, AF.Gelu, bias=b1g_sb[:, u:u + 1])
                        tt(h2T[u][:, hf * 512:(hf + 1) * 512], ag_bf, gg_sb, ALU.mult)
                dma(w2b[0], dw["w2_t"][l][0][:, 0:1408], "w")
                dma(w2b[1], dw["w2_t"][l][0][:, 1408:2816], "w")
                for gq in range(DCH):
                    for hf in range(2):
                        for ub in range(2):
                            for jj_ in range(11):
                                u = ub * 11 + jj_
                                mm(CTX, w2b[ub][:, jj_ * 128:(jj_ + 1) * 128],
                                   h2T[u][:, hf * 512:(hf + 1) * 512],
                                   u == 0, u == 21)
                        dst = xfT[gq][:, 1 + ms * 1024 + hf * 512:
                                      1 + ms * 1024 + (hf + 1) * 512]
                        ts(ag_sb, CTX, b2_sb[:, gq:gq + 1], None, ALU.add)
                        tt(dst, dst, ag_sb, ALU.add)
                    if gq + 1 < DCH:
                        dma(w2b[0], dw["w2_t"][l][gq + 1][:, 0:1408], "w")
                        dma(w2b[1], dw["w2_t"][l][gq + 1][:, 1408:2816], "w")
            if debug:
                dmadbg = dbg_tensor(f"xf_l{l}", (128, N + 1), F32)
                dma(dmadbg, xfT[0][:, :], "io")

        for t in range(NCH):
            xr = x1r_sb[t % 2]
            dma(xr, x1s_d[t * 128:(t + 1) * 128, :], "io")
            ob = outr_sb[t % 2]
            for c in range(DCH):
                slot = SCO_B[:, (c % 8) * 128:(c % 8) * 128 + 128]
                transp(slot, xfT[c][:, 1 + t * 128:1 + (t + 1) * 128], idf)
                tt(ob[:, c * 128:(c + 1) * 128], xr[:, c * 128:(c + 1) * 128],
                   slot, ALU.add)
            dma(out_d[t * 128:(t + 1) * 128, :], ob, "io")

        P.finalize()
        with nc.Block() as block:
            @block.tensor
            def _(e):
                P.emit("tensor", e, sems, dsems)

            @block.vector
            def _(e):
                P.emit("vector", e, sems, dsems)

            @block.scalar
            def _(e):
                P.emit("scalar", e, sems, dsems)

            @block.gpsimd
            def _(e):
                P.emit("gpsimd", e, sems, dsems)

            @block.sync
            def _(e):
                P.emit("sync", e, sems, dsems)
                for k, v in sorted(P.final_dma.items()):
                    e.wait_ge(dsems[k], v)
    return nc, dbg_out



_CACHED = {}

def _get_program(debug=False):
    key = bool(debug)
    if key not in _CACHED:
        _CACHED[key] = build_program(debug=debug)
    return _CACHED[key]


def kernel(**inputs):
    consts = host_constants()
    wl = host_weights(inputs)
    nc, _ = _get_program(debug=False)
    base = {("c_" + k): np.ascontiguousarray(v) for k, v in consts.items()}
    base.update({k: np.ascontiguousarray(v.astype(np.float32)) for k, v in wl.items()})
    X = inputs["L0_attn_hidden_states"]
    in_maps = []
    for b in range(8):
        m = dict(base)
        m["x"] = np.ascontiguousarray(X[b])
        in_maps.append(m)
    res = run_bass_kernel_spmd(nc, in_maps, core_ids=list(range(8)))
    out = np.stack([res.results[b]["out"] for b in range(8)], 0)
    return out.astype(np.float32)


# version 23
# speedup vs baseline: 1.1913x; 1.1913x over previous
import contextlib
import numpy as np
import ml_dtypes
import concourse.bass as bass
import concourse.mybir as mybir
from concourse.bass_utils import run_bass_kernel_spmd

F32 = mybir.dt.float32
BF16 = mybir.dt.bfloat16
AF = mybir.ActivationFunctionType
ALU = mybir.AluOpType
AX = mybir.AxisListType
BF = ml_dtypes.bfloat16

N = 2048
D = 1024
NCH = 16
DCH = 8
HID = 2730
NU = 22
DT_SZ = {F32: 4, BF16: 2}


def _outer_rope_tables():
    ids = np.arange(N)
    f_ids = ids // 256
    rem = ids - 256 * f_ids
    h_ids = rem // 16
    w_ids = rem - 16 * h_ids
    cos = np.zeros((N, 64), np.float64)
    sin = np.zeros((N, 64), np.float64)
    for c, pos in enumerate((f_ids, h_ids, w_ids)):
        omega = np.arange(10, dtype=np.float64) / 10.0
        inv = 1.0 / 10000.0 ** omega
        freq = pos[:, None].astype(np.float64) * inv
        cos[:, c * 20:(c + 1) * 20] = np.concatenate([np.cos(freq), np.cos(freq)], -1)
        sin[:, c * 20:(c + 1) * 20] = np.concatenate([np.sin(freq), np.sin(freq)], -1)
    cos[:, 60:] = 1.0
    sin[:, 60:] = 0.0
    R = np.zeros((64, 64), np.float64)
    for c in range(3):
        for p in range(10):
            d0, d1 = c * 20 + 2 * p, c * 20 + 2 * p + 1
            R[d0, d1] = -1.0
            R[d1, d0] = 1.0
    return cos, sin, R


def _inner_xpos_tables():
    half_idx = np.arange(0, 64, 2, dtype=np.float64)
    inv_freq = 1.0 / 10000.0 ** (half_idx / 64.0)
    t = np.arange(N, dtype=np.float64)
    freqs = t[:, None] * inv_freq[None, :]
    freqs = np.concatenate([freqs, freqs], -1)
    base = (half_idx + 25.6) / 89.6
    power = (t - N // 2) / 256.0
    scale = base[None, :] ** power[:, None]
    scale = np.concatenate([scale, scale], -1)
    return (np.cos(freqs) * scale, np.sin(freqs) * scale,
            np.cos(freqs) / scale, np.sin(freqs) / scale)


def _rot_half_mat():
    R = np.zeros((64, 64), np.float64)
    for d in range(32):
        R[d, d + 32] = -1.0
        R[d + 32, d] = 1.0
    return R


def host_constants():
    c = {}
    cosO, sinO, R_O = _outer_rope_tables()
    c["cosO"] = np.ascontiguousarray(cosO.T).astype(BF)
    c["sinO"] = np.ascontiguousarray(sinO.T).astype(BF)
    R_OT128 = np.zeros((128, 128), np.float64); R_OT128[:64, :64] = R_O.T
    c["R_OT"] = np.ascontiguousarray(R_OT128).astype(BF)
    cq, sq, ck, sk = _inner_xpos_tables()
    c["cq8"] = np.ascontiguousarray((8.0 * cq).T).astype(BF)
    c["sq8"] = np.ascontiguousarray((8.0 * sq).T).astype(BF)
    c["ckb"] = np.ascontiguousarray(ck.T).astype(BF)
    c["skb"] = np.ascontiguousarray(sk.T).astype(BF)
    R_IT128 = np.zeros((128, 128), np.float64); R_IT128[:64, :64] = _rot_half_mat().T
    c["R_IT"] = np.ascontiguousarray(R_IT128).astype(BF)
    pp = np.arange(128)[:, None]
    jj = np.arange(128)[None, :]
    c["Tsame"] = np.where(jj <= pp, 0.0, -1e30).astype(np.float32)
    c["Tprev"] = np.where(jj >= pp, 0.0, -1e30).astype(np.float32)
    c["Tboth"] = np.ascontiguousarray(np.concatenate(
        [c["Tprev"], c["Tsame"]], axis=1))
    c["idb"] = np.eye(128, dtype=BF)
    c["idf"] = np.eye(128, dtype=np.float32)
    c["ones64"] = np.ones((64, 1), BF)
    c["ones128"] = np.ones((128, 1), BF)
    c["ones_1x64"] = np.ones((1, 64), BF)
    c["ones_1x128"] = np.ones((1, 128), BF)
    c["onesf_1x128"] = np.ones((1, 128), np.float32)
    return c


def host_weights(w):
    o = {}
    def gt8(M):
        t = M.reshape(8, 128, 8, 128)
        return np.ascontiguousarray(t.transpose(2, 1, 0, 3).reshape(8, 128, 1024))
    o["wq_t"] = gt8(w["L0_attn_Wq"])
    o["wk_t"] = gt8(w["L0_attn_Wk"])
    o["wv_t"] = gt8(w["L0_attn_Wv"])
    o["wo_r"] = np.ascontiguousarray(w["L0_attn_Wo"].reshape(8, 128, 1024))
    o["bq_dm"] = np.ascontiguousarray(w["L0_attn_bq"].reshape(8, 128).T)
    o["bk_dm"] = np.ascontiguousarray(w["L0_attn_bk"].reshape(8, 128).T)
    o["bv_dm"] = np.ascontiguousarray(w["L0_attn_bv"].reshape(8, 128).T)
    o["bo_rep"] = np.ascontiguousarray(np.broadcast_to(w["L0_attn_bo"], (128, 1024)))

    wqkv = w["L0_ffn_wqkv"]
    t = wqkv[:, :, :1024].reshape(2, 8, 128, 8, 128)
    o["wqkv_qk"] = np.ascontiguousarray(t.transpose(0, 3, 2, 1, 4).reshape(2, 8, 128, 1024))
    o["wv_i"] = np.ascontiguousarray(wqkv[:, :, 1024:].reshape(2, 8, 128, 512))
    t = w["L0_ffn_wo"].reshape(2, 4, 128, 8, 128)
    o["wo_i"] = np.ascontiguousarray(t.transpose(0, 3, 2, 1, 4).reshape(2, 8, 128, 512))
    o["lng_dm"] = np.ascontiguousarray(w["L0_ffn_ln_g"].reshape(2, 8, 128).transpose(0, 2, 1))
    o["lnb_dm"] = np.ascontiguousarray(w["L0_ffn_ln_b"].reshape(2, 8, 128).transpose(0, 2, 1))

    w1 = w["L0_ffn_w1"]
    w1a = np.zeros((2, 1024, NU * 128), np.float32); w1a[:, :, :HID] = w1[:, :, :HID]
    w1g = np.zeros((2, 1024, NU * 128), np.float32); w1g[:, :, :HID] = w1[:, :, HID:]
    def ut(M):
        t = M.reshape(2, 8, 128, NU, 128)
        return np.ascontiguousarray(t.transpose(0, 3, 2, 1, 4).reshape(2, NU, 128, 1024))
    o["w1a_t"] = ut(w1a)
    o["w1g_t"] = ut(w1g)
    b1 = w["L0_ffn_b1"]
    b1a = np.zeros((2, NU * 128), np.float32); b1a[:, :HID] = b1[:, :HID]
    b1g = np.zeros((2, NU * 128), np.float32); b1g[:, :HID] = b1[:, HID:]
    o["b1a_dm"] = np.ascontiguousarray(b1a.reshape(2, NU, 128).transpose(0, 2, 1))
    o["b1g_dm"] = np.ascontiguousarray(b1g.reshape(2, NU, 128).transpose(0, 2, 1))
    w2p = np.zeros((2, NU * 128, 1024), np.float32)
    w2p[:, :HID, :] = w["L0_ffn_w2"]
    t = w2p.reshape(2, NU, 128, 8, 128)
    o["w2_t"] = np.ascontiguousarray(t.transpose(0, 3, 2, 1, 4).reshape(2, 8, 128, NU * 128))
    o["b2_dm"] = np.ascontiguousarray(w["L0_ffn_b2"].reshape(2, 8, 128).transpose(0, 2, 1))

    qs, ks = w["L0_ffn_q_scale"], w["L0_ffn_k_scale"]
    o["qs_c"] = np.ascontiguousarray(qs.reshape(2, 64, 1))
    o["qsr_c"] = np.ascontiguousarray(np.concatenate([qs[:, 32:], qs[:, :32]], 1).reshape(2, 64, 1))
    o["ks_c"] = np.ascontiguousarray(ks.reshape(2, 64, 1))
    o["ksr_c"] = np.ascontiguousarray(np.concatenate([ks[:, 32:], ks[:, :32]], 1).reshape(2, 64, 1))
    return o



ENGINES = ("tensor", "vector", "scalar", "gpsimd", "sync")

def _reg(ap):
    t = ap.tensor
    nm = t.name
    cls = type(t).__name__
    esz = DT_SZ.get(ap.dtype, 4)
    if cls == "DRamTensorHandle":
        return (nm, 0, 1 << 60)
    dims = list(ap.ap)
    if not dims:
        return (nm, 0, 1 << 60)
    pstep = dims[0][0]
    off = ap.offset % pstep if pstep > 0 else ap.offset
    reach = 1
    for stp, cnt in dims[1:]:
        reach += abs(stp) * (cnt - 1)
    lo, hi = off * esz, (off + reach) * esz
    if cls == "PSumTensorHandle":
        lo = (lo // 2048) * 2048
        hi = ((hi + 2047) // 2048) * 2048
    return (nm, lo, hi)


class Prog:
    def __init__(self):
        self.recs = []
        self.eng = {e: [] for e in ENGINES}
        self.wlog = {}
        self.rlog = {}

    def op(self, engine, fn, reads=(), writes=(), dma=None):
        oid = len(self.recs)
        deps = set()
        rregs = [_reg(a) for a in reads]
        wregs = [_reg(a) for a in writes]
        for nm, lo, hi in rregs:
            for l2, h2, o2 in self.wlog.get(nm, ()):
                if lo < h2 and l2 < hi:
                    deps.add(o2)
        for nm, lo, hi in wregs:
            for l2, h2, o2 in self.wlog.get(nm, ()):
                if lo < h2 and l2 < hi:
                    deps.add(o2)
            for l2, h2, o2 in self.rlog.get(nm, ()):
                if lo < h2 and l2 < hi:
                    deps.add(o2)
        rec = dict(id=oid, engine=engine, fn=fn, dma=dma, deps=deps, needs_inc=False)
        self.recs.append(rec)
        self.eng[engine].append(rec)
        for nm, lo, hi in rregs:
            self.rlog.setdefault(nm, []).append((lo, hi, oid))
        for nm, lo, hi in wregs:
            lst = self.wlog.setdefault(nm, [])
            lst[:] = [e for e in lst if not (lo <= e[0] and e[1] <= hi)]
            lst.append((lo, hi, oid))
            rl = self.rlog.get(nm)
            if rl:
                rl[:] = [e for e in rl if not (lo <= e[0] and e[1] <= hi)]
        return oid

    NDSEM = 24

    def finalize(self):
        dix = {"w": 0, "io": 0}
        for rec in self.recs:
            if rec["dma"] is not None:
                s = rec["dma"]
                rec["dslot"] = (s, dix[s] % self.NDSEM)
                dix[s] += 1
        for rec in self.recs:
            per = {}
            for d in rec["deps"]:
                p = self.recs[d]
                if (p["engine"] == "tensor" and rec["engine"] == "tensor"
                        and p["dma"] is None and rec["dma"] is None):
                    continue
                key = ("dma", p["dslot"]) if p["dma"] is not None else ("eng", p["engine"])
                if per.get(key, -1) < d:
                    per[key] = d
            rec["waits"] = per
            for d in per.values():
                self.recs[d]["needs_inc"] = True
        cnt = {e: 0 for e in ENGINES}
        dcnt = {}
        for rec in self.recs:
            if rec["dma"] is not None:
                k = rec["dslot"]
                dcnt[k] = dcnt.get(k, 0) + 16
                rec["done_val"] = dcnt[k]
                rec["needs_inc"] = True
                rec["self_wait"] = dcnt[k] - 16 if dcnt[k] > 16 else None
            elif rec["needs_inc"]:
                cnt[rec["engine"]] += 1
                rec["done_val"] = cnt[rec["engine"]]
            else:
                rec["done_val"] = None
        self.final_dma = dcnt

    def emit(self, engine, eh, sems, dsems):
        wm = {}
        for rec in self.eng[engine]:
            for key, d in sorted(rec["waits"].items(), key=lambda kv: kv[1]):
                v = self.recs[d]["done_val"]
                if wm.get(key, -1) >= v:
                    continue
                sem = dsems[key[1]] if key[0] == "dma" else sems[key[1]]
                eh.wait_ge(sem, v)
                wm[key] = v
            if rec["dma"] is not None and rec.get("self_wait"):
                k2 = ("dma", rec["dslot"])
                if wm.get(k2, -1) < rec["self_wait"]:
                    eh.wait_ge(dsems[rec["dslot"]], rec["self_wait"])
                    wm[k2] = rec["self_wait"]
            ins = rec["fn"](eh)
            if rec["dma"] is not None:
                ins.then_inc(dsems[rec["dslot"]], 16)
            elif rec["needs_inc"]:
                ins.then_inc(sems[rec["engine"]], 1)



def build_program(debug=False):
    nc = bass.Bass("TRN2", target_bir_lowering=False)
    P = Prog()

    x_d = nc.dram_tensor("x", [N, D], F32, kind="ExternalInput").ap()
    out_d = nc.dram_tensor("out", [N, D], F32, kind="ExternalOutput").ap()
    kscr = "ExternalOutput" if debug else "Internal"
    x1s_d = nc.dram_tensor("x1s", [N, D], F32, kind=kscr).ap()
    ctxs_d = nc.dram_tensor("ctxs", [D, N], BF16, kind=kscr).ap()

    consts = host_constants()
    dconst = {nm: nc.dram_tensor("c_" + nm, list(a.shape),
                                 BF16 if a.dtype == BF else F32,
                                 kind="ExternalInput").ap()
              for nm, a in consts.items()}
    wshapes = {
        "wq_t": (8, 128, 1024), "wk_t": (8, 128, 1024), "wv_t": (8, 128, 1024),
        "wo_r": (8, 128, 1024), "bq_dm": (128, 8), "bk_dm": (128, 8),
        "bv_dm": (128, 8), "bo_rep": (128, 1024),
        "wqkv_qk": (2, 8, 128, 1024), "wv_i": (2, 8, 128, 512),
        "wo_i": (2, 8, 128, 512), "lng_dm": (2, 128, 8), "lnb_dm": (2, 128, 8),
        "w1a_t": (2, NU, 128, 1024), "w1g_t": (2, NU, 128, 1024),
        "b1a_dm": (2, 128, NU), "b1g_dm": (2, 128, NU),
        "w2_t": (2, 8, 128, NU * 128), "b2_dm": (2, 128, 8),
        "qs_c": (2, 64, 1), "qsr_c": (2, 64, 1), "ks_c": (2, 64, 1), "ksr_c": (2, 64, 1),
    }
    dw = {nm: nc.dram_tensor(nm, list(sh), F32, kind="ExternalInput").ap()
          for nm, sh in wshapes.items()}

    dbg_out = {}
    def dbg_tensor(nm, shape, dt):
        ap = nc.dram_tensor("dbg_" + nm, list(shape), dt, kind="ExternalOutput").ap()
        dbg_out[nm] = ap
        return ap

    stack = contextlib.ExitStack()
    S = lambda name, shape, dt: stack.enter_context(nc.sbuf_tensor(name, shape, dt)).ap()
    PSU = lambda name, shape, dt: stack.enter_context(nc.psum_tensor(name, shape, dt)).ap()

    with stack:
        SCO_A = PSU("SCO_A", [128, 1024], F32)
        SCO_B = PSU("SCO_B", [128, 1024], F32)
        PSX = PSU("PSX", [128, 1024], F32)
        TRP = PSX.bitcast(BF16)
        CTX = PSU("CTX", [128, 512], F32)
        AUX = PSU("AUX", [128, 512], F32)

        xfT = [S(f"xfT{c}", [128, N + 1], F32) for c in range(DCH)]
        cq8 = S("cq8", [64, N], BF16); sq8 = S("sq8", [64, N], BF16)
        ckb = S("ckb", [64, N], BF16); skb = S("skb", [64, N], BF16)
        Tboth = S("Tboth", [128, 256], F32)
        Tprev = Tboth[:, 0:128]
        Tsame = Tboth[:, 128:256]
        idb = S("idb", [128, 128], BF16); idf = S("idf", [128, 128], F32)
        R_OT = S("R_OT", [128, 128], BF16); R_IT = S("R_IT", [128, 128], BF16)
        ones64 = S("ones64", [64, 1], BF16)
        ones128 = S("ones128", [128, 1], BF16)
        ones_1x64 = S("o1x64", [1, 64], BF16)
        ones_1x128 = S("o1x128", [1, 128], BF16)
        onesf_1x128 = S("of1x128", [1, 128], F32)
        bq_sb = S("bq_sb", [128, 8], F32); bk_sb = S("bk_sb", [128, 8], F32)
        bv_sb = S("bv_sb", [128, 8], F32)
        b1a_sb = S("b1a_sb", [128, NU], F32); b1g_sb = S("b1g_sb", [128, NU], F32)
        b2_sb = S("b2_sb", [128, 8], F32)
        lng_sb = S("lng_sb", [128, 8], F32); lnb_sb = S("lnb_sb", [128, 8], F32)
        qs_sb = S("qs_sb", [64, 1], F32); qsr_sb = S("qsr_sb", [64, 1], F32)
        ks_sb = S("ks_sb", [64, 1], F32); ksr_sb = S("ksr_sb", [64, 1], F32)

        hT = [S(f"hT{c}", [128, N], BF16) for c in range(DCH)]
        Bp = [S(f"Bp{i}", [128, N], BF16) for i in range(4)]
        QKt = [S(f"QK{i}", [128, N], BF16) for i in range(4)]
        QK = [q[0:64, :] for q in QKt]
        stg = [S(f"stg{c}", [128, 513], BF16) for c in range(DCH)]
        va = [S(f"va{i}", [128, 16 * 65 + 64], BF16) for i in range(2)]
        E_A = S("E_A", [128, 1024], BF16); E_B = S("E_B", [128, 1024], BF16)
        sq_sc = S("sq_sc", [64, N], BF16)
        wqb = [S(f"wqb{i}", [128, 1024], BF16) for i in range(2)]
        wkb = [S(f"wkb{i}", [128, 1024], BF16) for i in range(2)]
        wvb = [S(f"wvb{i}", [128, 1024], BF16) for i in range(2)]
        w2b = [S(f"w2b{i}", [128, 11 * 128], BF16) for i in range(2)]
        h0_sb = S("h0_sb", [128, 1024], F32)
        ag_sb = S("ag_sb", [128, 512], F32)
        TMP = S("TMP", [128, 512], F32)
        st_sc = S("st_sc", [128, 16], F32)
        sv13 = S("sv13", [128, 8], F32)
        red_sc = S("red_sc", [1, 8], F32)
        negB = [S(f"negB{i}", [128, 1], F32) for i in range(2)]
        rn_bf = S("rn_bf", [1, 512], BF16)
        rd_bf = rn_bf
        nb_bf = rn_bf[0:1, 0:1]
        mu_s = S("mu_s", [1, 512], F32); rs_s = S("rs_s", [1, 512], F32)
        den_i = [S(f"den_i{j}", [128, 1], F32) for j in range(2)]
        rden = [S(f"rden{j}", [128, 1], F32) for j in range(2)]
        negm = [S(f"negm{j}", [128, 1], F32) for j in range(2)]
        trow = S("trow", [1, 512], F32)
        rdb_sb = S("rdb", [64, 512], F32)
        tmp_a = rdb_sb
        tmp_b = S("tmpb", [64, 512], F32)
        cbo = [S(f"cbo{i}", [64, 512], BF16) for i in range(2)]
        cosO = Bp[3][0:64, :]
        sinO = Bp[2][0:64, :]
        bo_rep = Bp[2].bitcast(F32)
        gg_sb = TMP.bitcast(BF16)[:, 0:512]
        ag_bf = TMP.bitcast(BF16)[:, 512:1024]
        E_rawb = TMP.bitcast(BF16)[:, 0:640]
        woib = [wvb[i][:, 0:512] for i in range(2)]

        wo_sb = [hT[c // 2][:, (c % 2) * 1024:(c % 2 + 1) * 1024] for c in range(8)]
        v_i = [hT[c // 4][:, (c % 4) * 512:(c % 4 + 1) * 512] for c in range(16)]
        ctxT_i = [hT[4 + i] for i in range(4)]
        wvi = [hT[4 + c // 4][:, (c % 4) * 512:(c % 4 + 1) * 512] for c in range(8)]
        x0_sb = [hT[4].bitcast(F32), hT[5].bitcast(F32)]
        x1w_sb = [hT[6].bitcast(F32), hT[7].bitcast(F32)]
        h2T = ([hT[c // 2][:, (c % 2) * 1024:(c % 2 + 1) * 1024] for c in range(16)]
               + [va[0][:, 0:1024], va[1][:, 0:1024], E_A[:, :], E_B[:, :],
                  QKt[0][:, 0:1024], QKt[1][:, 0:1024]])
        x1r_sb = [hT[0].bitcast(F32), hT[1].bitcast(F32)]
        outr_sb = [hT[2].bitcast(F32), hT[3].bitcast(F32)]
        xin_sb = [Bp[1].bitcast(F32), Bp[2].bitcast(F32)]
        h_sb = [Bp[0][:, 0:1024], Bp[0][:, 1024:2048]]
        vT_sb = Bp[1]
        csl = ([Bp[0][:, i * 512:(i + 1) * 512] for i in range(4)]
               + [Bp[1][:, i * 512:(i + 1) * 512] for i in range(4)])
        sqb_i = Bp[2][:, 0:1024]
        En_i = [Bp[2][:, 1024:1664], Bp[3][:, 0:640]]
        ET_i = [Bp[3][:, 640:1280], Bp[3][:, 1280:1920]]
        _hnT_homes = [Bp[0], Bp[1], Bp[3], QKt[2]]
        hnT = [_hnT_homes[c // 2][:, (c % 2) * 1024:(c % 2 + 1) * 1024] for c in range(8)]
        rnb = [va[0].bitcast(F32)[0:64, 0:512], va[1].bitcast(F32)[0:64, 0:512]]

        sems = {e: stack.enter_context(nc.semaphore(f"sem_{e}")) for e in ENGINES}
        dsems = {}
        for k in ("w", "io"):
            for j in range(Prog.NDSEM):
                dsems[(k, j)] = stack.enter_context(nc.semaphore(f"dma_{k}{j}"))

        def dma(dst, src, stream, eng=None):
            eng = eng or ("gpsimd" if stream == "w" else "sync")
            P.op(eng, lambda e, d=dst, s=src: e.dma_start(d, s),
                 reads=[src], writes=[dst], dma=stream)

        def mm(out, lhsT, rhs, start, stop):
            P.op("tensor", lambda e, o=out, l=lhsT, r=rhs, a=start, b=stop:
                 e.matmul(o, l, r, start=a, stop=b), reads=[lhsT, rhs], writes=[out])

        def transp(out, in_, ident):
            P.op("tensor", lambda e, o=out, i=in_, d=ident: e.transpose(o, i, d),
                 reads=[in_, ident], writes=[out])

        def tt(out, a, b, op):
            P.op("vector", lambda e, o=out, x=a, y=b, p=op: e.tensor_tensor(o, x, y, p),
                 reads=[a, b], writes=[out])

        def ts(out, in0, s1, s2, op0, op1=ALU.bypass):
            rd = [in0] + [s for s in (s1, s2) if hasattr(s, "tensor")]
            P.op("vector", lambda e, o=out, i=in0, a=s1, b=s2, p0=op0, p1=op1:
                 e.tensor_scalar(o, i, a, b, p0, p1), reads=rd, writes=[out])

        def stt(out, in0, s, in1, op0, op1):
            rd = [in0, in1] + ([s] if hasattr(s, "tensor") else [])
            P.op("vector", lambda e, o=out, a=in0, c=s, b=in1, p0=op0, p1=op1:
                 e.scalar_tensor_tensor(o, a, c, b, p0, p1), reads=rd, writes=[out])

        def act(out, in_, func, bias=0.0, scale=1.0, accum=None):
            rd = [in_] + [b for b in (bias, scale) if hasattr(b, "tensor")]
            wr = [out] + ([accum] if accum is not None else [])
            P.op("scalar", lambda e, o=out, i=in_, f=func, b=bias, s=scale, a=accum:
                 e.activation(o, i, f, bias=b, scale=s, accum_out=a), reads=rd, writes=wr)

        def red(out, in_, op, negate=False):
            P.op("vector", lambda e, o=out, i=in_, p=op, n=negate:
                 e.tensor_reduce(o, i, AX.X, p, negate=n), reads=[in_], writes=[out])

        def cp(out, in_, eng="vector"):
            if eng == "scalar":
                P.op(eng, lambda e, o=out, i=in_: e.copy(o, i), reads=[in_], writes=[out])
            else:
                P.op(eng, lambda e, o=out, i=in_: e.tensor_copy(o, i),
                     reads=[in_], writes=[out])

        def memset(ap, v, eng="gpsimd"):
            P.op(eng, lambda e, a=ap, c=v: e.memset(a, c), reads=[], writes=[ap])

        def recip(out, in_):
            P.op("vector", lambda e, o=out, i=in_: e.reciprocal(o, i),
                 reads=[in_], writes=[out])

        for nm, sb in (("cosO", cosO), ("R_OT", R_OT), ("R_IT", R_IT),
                       ("cq8", cq8), ("sq8", sq8), ("ckb", ckb), ("skb", skb),
                       ("Tboth", Tboth), ("idb", idb), ("idf", idf),
                       ("ones64", ones64), ("ones128", ones128),
                       ("ones_1x64", ones_1x64), ("ones_1x128", ones_1x128),
                       ("onesf_1x128", onesf_1x128)):
            dma(sb, dconst[nm], "io")
        dma(bq_sb, dw["bq_dm"], "io"); dma(bk_sb, dw["bk_dm"], "io")
        dma(bv_sb, dw["bv_dm"], "io")
        for c in range(DCH):
            memset(xfT[c][:, 0:1], 0.0)
        for i in range(2):
            memset(va[i][:, 64::65], 1.0, eng="vector")
            memset(va[i][:, 1040:1104], 0.0, eng="vector")
        for i in range(4):
            memset(QKt[i][64:128, :], 0.0, eng="vector")

        def ln_rowstats(x_in, width, rb=0):
            c = lambda j: st_sc[:, rb + j:rb + j + 1]
            red(c(4), x_in, ALU.add)
            act(SCO_B[:, 0:width], x_in, AF.Square, accum=c(5))
            ts(c(0), c(4), 1.0 / width, None, ALU.mult)
            tt(c(1), c(0), c(0), ALU.mult)
            stt(c(2), c(5), 1.0 / width, c(1), ALU.mult, ALU.subtract)
            ts(c(7), c(2), 1e-5, None, ALU.add)
            act(c(6), c(7), AF.Ln)
            act(c(3), c(6), AF.Exp, scale=-0.5)

        for t in range(NCH):
            xb = xin_sb[t % 2]
            dma(xb, x_d[t * 128:(t + 1) * 128, :], "io")
            rb = 8 * (t % 2)
            ln_rowstats(xb, D, rb)
            hb = h_sb[t % 2]
            ts(hb, xb, st_sc[:, rb:rb + 1], st_sc[:, rb + 3:rb + 4],
               ALU.subtract, ALU.mult)
            for c in range(DCH):
                slot = TRP[:, c * 256:c * 256 + 128]
                transp(slot, hb[:, c * 128:(c + 1) * 128], idb)
                cp(hT[c][:, t * 128:(t + 1) * 128], slot, eng="scalar")

        if debug:
            dma(dbg_tensor("hT0", (128, N), BF16), hT[0][:, :], "io")
        dma(sinO, dconst["sinO"], "io")

        for g in range(2):
            dma(wqb[g], dw["wq_t"][g], "w")
            dma(wkb[g], dw["wk_t"][g], "w")
            dma(wvb[g], dw["wv_t"][g], "w")
        qa, qb_, ka, kb_ = QK

        for g in range(DCH):
            wq, wk, wv = wqb[g % 2], wkb[g % 2], wvb[g % 2]
            bq0, bq1 = bq_sb[0:64, g:g + 1], bq_sb[64:128, g:g + 1]
            bk0, bk1 = bk_sb[0:64, g:g + 1], bk_sb[64:128, g:g + 1]
            for s in range(4):
                sl = slice(s * 512, (s + 1) * 512)
                q_ps = SCO_A[:, (s % 2) * 512:(s % 2) * 512 + 512]
                k_ps = SCO_B[:, (s % 2) * 512:(s % 2) * 512 + 512]
                for c in range(DCH):
                    mm(q_ps, wq[:, c * 128:(c + 1) * 128], hT[c][:, sl], c == 0, c == 7)
                for c in range(DCH):
                    mm(k_ps, wk[:, c * 128:(c + 1) * 128], hT[c][:, sl], c == 0, c == 7)
                for c in range(DCH):
                    mm(CTX, wv[:, c * 128:(c + 1) * 128], hT[c][:, sl], c == 0, c == 7)
                act(qa[:, sl], q_ps[0:64, :], AF.Identity, bias=bq0)
                act(qb_[:, sl], q_ps[64:128, :], AF.Identity, bias=bq1)
                act(ka[:, sl], k_ps[0:64, :], AF.Identity, bias=bk0)
                act(kb_[:, sl], k_ps[64:128, :], AF.Identity, bias=bk1)
                act(vT_sb[:, sl], CTX, AF.Identity, bias=bv_sb[:, g:g + 1])
            if debug and g == 0:
                dma(dbg_tensor("qn0", (64, N), BF16), qa[:, :], "io")
                dma(dbg_tensor("kn0", (64, N), BF16), ka[:, :], "io")
                dma(dbg_tensor("vT0", (128, N), BF16), vT_sb[:, :], "io")
            for t in range(NCH):
                slot = TRP[:, (t % 8) * 256:(t % 8) * 256 + 128]
                transp(slot, vT_sb[:, t * 128:(t + 1) * 128], idb)
                cp(va[0][:, t * 65:t * 65 + 64], slot[:, 0:64])
                cp(va[1][:, t * 65:t * 65 + 64], slot[:, 64:128])
            if g == 7:
                for c in range(DCH):
                    dma(wo_sb[c], dw["wo_r"][c], "w")
            for i in range(2):
                qv = (qa, qb_)[i]
                kv = (ka, kb_)[i]
                qvp = (QKt[0], QKt[1])[i]
                kvp = (QKt[2], QKt[3])[i]
                for s in range(4):
                    sl = slice(s * 512, (s + 1) * 512)
                    mm(AUX[:, :], R_OT, qvp[:, sl], True, True)
                    tt(tmp_a, qv[:, sl], cosO[:, sl], ALU.mult)
                    tt(tmp_b, AUX[0:64, :], sinO[:, sl], ALU.mult)
                    tt(qv[:, sl], tmp_a, tmp_b, ALU.add)
                    mm(AUX[:, :], R_OT, kvp[:, sl], True, True)
                    tt(tmp_a, kv[:, sl], cosO[:, sl], ALU.mult)
                    tt(tmp_b, AUX[0:64, :], sinO[:, sl], ALU.mult)
                    tt(kv[:, sl], tmp_a, tmp_b, ALU.add)
                tt(sq_sc, qv, qv, ALU.mult)
                for s in range(4):
                    pb = (SCO_A, SCO_B)[s % 2][0:1, (s // 2) * 512:(s // 2) * 512 + 512]
                    mm(pb, ones64, sq_sc[:, s * 512:(s + 1) * 512], True, True)
                    red(red_sc[:, s:s + 1], pb, ALU.max)
                red(red_sc[:, 4:5], red_sc[:, 0:4], ALU.max)
                tt(sq_sc, kv, kv, ALU.mult)
                for s in range(4):
                    pb = (SCO_A, SCO_B)[s % 2][0:1, (s // 2) * 512:(s // 2) * 512 + 512]
                    mm(pb, ones64, sq_sc[:, s * 512:(s + 1) * 512], True, True)
                    red(red_sc[:, s:s + 1], pb, ALU.max)
                red(red_sc[:, 5:6], red_sc[:, 0:4], ALU.max)
                tt(red_sc[:, 6:7], red_sc[:, 4:5], red_sc[:, 5:6], ALU.mult)
                act(red_sc[:, 7:8], red_sc[:, 6:7], AF.Ln)
                act(red_sc[:, 6:7], red_sc[:, 7:8], AF.Exp, scale=0.5)
                ts(nb_bf, red_sc[:, 6:7], -0.125, None, ALU.mult)
                mm(AUX[:, 0:1], ones_1x128, nb_bf, True, True)
                cp(negB[i], AUX[:, 0:1], eng="scalar")
                if debug and 2 * g + i == 0:
                    dma(dbg_tensor("qf0", (64, N), BF16), qv[:, :], "io")
                    dma(dbg_tensor("kf0", (64, N), BF16), kv[:, :], "io")
                    dma(dbg_tensor("negB0", (128, 1), F32), negB[i][:, :], "io")
                    dma(dbg_tensor("va0", (128, 1104), BF16), va[0][:, :], "io")
            QVp = (QKt[0], QKt[1]); KVp = (QKt[2], QKt[3])
            SCs = (SCO_A, SCO_B)
            Es = ((E_A, E_B), (h0_sb.bitcast(BF16)[:, 0:1024],
                               h0_sb.bitcast(BF16)[:, 1024:2048]))
            CTXs = (CTX, PSX[:, 0:512])
            for s in range(4):
                qsl = slice(s * 512, (s + 1) * 512)
                def emit_ctx2(i, b):
                    EE = Es[i][b % 2]
                    for j in range(2):
                        kc = 2 * b + j
                        mm(CTXs[i], va[i][:, kc * 65:kc * 65 + 128],
                           EE[:, j * 512:(j + 1) * 512], kc == 0, kc == 15)
                for b in range(8):
                    for i in range(2):
                        for j in range(2):
                            kc = 2 * b + j
                            mm(SCs[i][:, j * 512:(j + 1) * 512],
                               KVp[i][:, kc * 128:(kc + 1) * 128],
                               QVp[i][:, qsl], True, True)
                    for i in range(2):
                        act(Es[i][b % 2], SCs[i], AF.Exp, bias=negB[i], scale=0.125)
                    if b > 0:
                        emit_ctx2(0, b - 1)
                        emit_ctx2(1, b - 1)
                emit_ctx2(0, 7)
                emit_ctx2(1, 7)
                for i in range(2):
                    h = 2 * g + i
                    act(trow, CTXs[i][64:65, :], AF.Ln)
                    act(rd_bf, trow, AF.Exp, scale=-1.0)
                    mm(PSX[0:64, 512:1024], ones_1x64, rd_bf, True, True)
                    cp(rdb_sb, PSX[0:64, 512:1024], eng="scalar")
                    tt(cbo[i], CTXs[i][0:64, :], rdb_sb, ALU.mult)
                    dma(ctxs_d[h * 64:(h + 1) * 64, s * 512:(s + 1) * 512],
                        cbo[i], "io")

            if g + 2 < DCH:
                dma(wqb[g % 2], dw["wq_t"][g + 2], "w")
                dma(wkb[g % 2], dw["wk_t"][g + 2], "w")
                dma(wvb[g % 2], dw["wv_t"][g + 2], "w")

        dma(bo_rep, dw["bo_rep"], "io")
        for tb in range(4):
            for c in range(DCH):
                dma(csl[c], ctxs_d[c * 128:(c + 1) * 128, tb * 512:(tb + 1) * 512], "io")
            for tq in range(4):
                t = tb * 4 + tq
                xb = x0_sb[tq % 2]
                dma(xb, x_d[t * 128:(t + 1) * 128, :], "io")
                x1b = x1w_sb[tq % 2]
                for ns in range(2):
                    ps = (SCO_A, SCO_B)[ns][:, 0:512]
                    for c in range(DCH):
                        mm(ps, csl[c][:, tq * 128:(tq + 1) * 128],
                           wo_sb[c][:, ns * 512:(ns + 1) * 512], c == 0, c == 7)
                    sl = slice(ns * 512, (ns + 1) * 512)
                    tt(x1b[:, sl], ps, bo_rep[:, sl], ALU.add)
                tt(x1b, x1b, xb, ALU.add)
                dma(x1s_d[t * 128:(t + 1) * 128, :], x1b, "io")
                rb = 8 * (t % 2)
                ln_rowstats(x1b, D, rb)
                ts(h0_sb, x1b, st_sc[:, rb:rb + 1], st_sc[:, rb + 3:rb + 4],
                   ALU.subtract, ALU.mult)
                for c in range(DCH):
                    slot = SCO_A[:, c * 128:(c + 1) * 128]
                    transp(slot, h0_sb[:, c * 128:(c + 1) * 128], idf)
                    cp(xfT[c][:, 1 + t * 128:1 + (t + 1) * 128], slot)

        def restage(sl):
            for c in range(DCH):
                cp(stg[c], xfT[c][:, sl * 512:sl * 512 + 513], eng="gpsimd")

        def stg_sh(c, lo, width):
            off = lo if c >= 4 else lo + 1
            return stg[c][:, off:off + width]

        def xf_sh(c, sl, width=512):
            off = sl * 512 if c >= 4 else sl * 512 + 1
            return xfT[c][:, off:off + width]

        for l in range(2):
            dma(qs_sb, dw["qs_c"][l], "io"); dma(qsr_sb, dw["qsr_c"][l], "io")
            dma(ks_sb, dw["ks_c"][l], "io"); dma(ksr_sb, dw["ksr_c"][l], "io")
            dma(b1a_sb, dw["b1a_dm"][l], "io"); dma(b1g_sb, dw["b1g_dm"][l], "io")
            dma(b2_sb, dw["b2_dm"][l], "io")
            dma(lng_sb, dw["lng_dm"][l], "io"); dma(lnb_sb, dw["lnb_dm"][l], "io")
            for c in range(DCH):
                dma(wvi[c], dw["wv_i"][l][c], "w")
            for sl in range(4):
                restage(sl)
                for j in range(4):
                    tch = sl * 4 + j
                    for c in range(DCH):
                        mm(CTX, stg_sh(c, j * 128, 128), wvi[c], c == 0, c == 7)
                    cp(v_i[tch], CTX)
            for i4 in range(4):
                memset(QKt[i4][64:128, :], 0.0, eng="vector")
            for p in range(2):
                dma(wqb[p], dw["wqkv_qk"][l][p], "w")
                dma(wkb[p], dw["wqkv_qk"][l][4 + p], "w")
            chain_banks = [CTX, AUX, PSX[:, 0:512], PSX[:, 512:1024]]
            sq_slots = [Bp[2][0:64, j * 512:(j + 1) * 512] for j in range(4)]
            mu_slots = [mu_s, rs_s, trow, h0_sb[0:1, 0:512]]
            rn_slots = [rn_bf, cbo[0][0:1, 0:512], cbo[1][0:1, 0:512],
                        TMP.bitcast(BF16)[0:1, 0:512]]
            rnb_slots = [rnb[0], rnb[1], sq_sc.bitcast(F32)[:, 0:512],
                         sq_sc.bitcast(F32)[:, 512:1024]]

            def proj_emit(p_, sl_):
                wqp, wkp = wqb[p_ % 2], wkb[p_ % 2]
                q_ps = SCO_A[:, (sl_ % 2) * 512:(sl_ % 2) * 512 + 512]
                k_ps = SCO_B[:, (sl_ % 2) * 512:(sl_ % 2) * 512 + 512]
                for c in range(DCH):
                    mm(q_ps, wqp[:, c * 128:(c + 1) * 128], stg_sh(c, 0, 512),
                       c == 0, c == 7)
                for c in range(DCH):
                    mm(k_ps, wkp[:, c * 128:(c + 1) * 128], stg_sh(c, 0, 512),
                       c == 0, c == 7)
                if sl_ + 1 < 4:
                    restage(sl_ + 1)

            for p in range(4):
                restage(0)
                proj_emit(p, 0)
                for sl in range(4):
                    slc = slice(sl * 512, (sl + 1) * 512)
                    q_ps = SCO_A[:, (sl % 2) * 512:(sl % 2) * 512 + 512]
                    k_ps = SCO_B[:, (sl % 2) * 512:(sl % 2) * 512 + 512]
                    chains = []
                    for i in range(2):
                        rows = slice(i * 64, (i + 1) * 64)
                        for wq_, (ps_, outv, outvp, csT, snT, sc_, scr_) in (
                                (0, (q_ps, (qa, qb_)[i], (QKt[0], QKt[1])[i],
                                     cq8, sq8, qs_sb, qsr_sb)),
                                (1, (k_ps, (ka, kb_)[i], (QKt[2], QKt[3])[i],
                                     ckb, skb, ks_sb, ksr_sb))):
                            chains.append(dict(rows=rows, ps=ps_, outv=outv,
                                               outvp=outvp, csT=csT, snT=snT,
                                               sc=sc_, scr=scr_,
                                               bank=chain_banks[2 * wq_ + i],
                                               sq=sq_slots[2 * wq_ + i],
                                               mu=mu_slots[2 * wq_ + i],
                                               rn=rn_slots[2 * wq_ + i],
                                               rnb=rnb_slots[2 * wq_ + i]))
                    for ch in chains:
                        act(ch["sq"], ch["ps"][ch["rows"], :], AF.Square)
                    for ch in chains:
                        mm(ch["bank"][0:1, :], ones64, ch["sq"], True, True)
                    for ch in chains:
                        act(ch["mu"], ch["bank"][0:1, :], AF.Ln)
                    for ch in chains:
                        act(ch["rn"], ch["mu"], AF.Exp, scale=-0.5)
                    if sl + 1 < 4:
                        proj_emit(p, sl + 1)
                    for ch in chains:
                        mm(ch["bank"][0:64, :], ones_1x64, ch["rn"], True, True)
                    for ch in chains:
                        cp(ch["rnb"][0:64, :], ch["bank"][0:64, :], eng="scalar")
                    for ch in chains:
                        tt(ch["outv"][:, slc], ch["ps"][ch["rows"], :],
                           ch["rnb"][0:64, :], ALU.mult)
                    for ch in chains:
                        mm(ch["bank"][:, :] if ch["bank"].partition_size() == 128
                           else ch["bank"], R_IT, ch["outvp"][:, slc], True, True)
                    for ch in chains:
                        stt(tmp_a, ch["outv"][:, slc], ch["sc"], ch["csT"][:, slc],
                            ALU.mult, ALU.mult)
                        stt(tmp_b, ch["bank"][0:64, :], ch["scr"], ch["snT"][:, slc],
                            ALU.mult, ALU.mult)
                        tt(ch["outv"][:, slc], tmp_a, tmp_b, ALU.add)
                E_raw2 = [E_rawb, h0_sb.bitcast(BF16)[:, 0:640]]

                def emit_tail(i, qc, z):
                    h = 2 * p + i
                    lo = max(0, qc - 4)
                    nb = qc + 1 - lo
                    width = nb * 128
                    for kk in range(nb):
                        slot = TRP[:, (z % 2) * 1024 + kk * 128:
                                   (z % 2) * 1024 + (kk + 1) * 128]
                        transp(slot, En_i[z % 2][:, kk * 128:(kk + 1) * 128], idb)
                    cp(ET_i[z % 2][:, 0:width],
                       TRP[:, (z % 2) * 1024:(z % 2) * 1024 + width],
                       eng=("scalar", "vector")[z % 2])
                    for kk in range(nb):
                        mm(CTX[:, (z % 4) * 128:(z % 4 + 1) * 128],
                           v_i[lo + kk][:, (h // 2) * 128:(h // 2) * 128 + 128],
                           ET_i[z % 2][:, kk * 128:(kk + 1) * 128],
                           kk == 0, kk == nb - 1)
                    cp(ctxT_i[p][i * 64:(i + 1) * 64, qc * 128:(qc + 1) * 128],
                       CTX[(h % 2) * 64:(h % 2) * 64 + 64,
                           (z % 4) * 128:(z % 4 + 1) * 128],
                       eng=("vector", "scalar")[z % 2])

                pend = []
                zs = [(qc, i) for qc in range(NCH) for i in range(2)]
                for z0 in range(0, len(zs), 2):
                    grp = []
                    for dz in range(2):
                        z = z0 + dz
                        qc, i = zs[z]
                        lo = max(0, qc - 4)
                        grp.append(dict(z=z, qc=qc, i=i, lo=lo,
                                        nb=qc + 1 - lo, base=lo * 128,
                                        width=(qc + 1 - lo) * 128,
                                        SC=(SCO_A, SCO_B)[z % 2],
                                        qvp=(QKt[0], QKt[1])[i],
                                        kvp=(QKt[2], QKt[3])[i]))
                    for g_ in grp:
                        w1_ = min(g_["width"], 512)
                        mm(g_["SC"][:, 0:w1_],
                           g_["qvp"][:, g_["qc"] * 128:(g_["qc"] + 1) * 128],
                           g_["kvp"][:, g_["base"]:g_["base"] + w1_], True, True)
                        if g_["width"] > 512:
                            mm(g_["SC"][:, 512:g_["width"]],
                               g_["qvp"][:, g_["qc"] * 128:(g_["qc"] + 1) * 128],
                               g_["kvp"][:, g_["base"] + 512:g_["base"] + g_["width"]],
                               True, True)
                    for g_ in grp:
                        if g_["qc"] >= 4:
                            scv = g_["SC"][:, 0:g_["width"]].rearrange(
                                "p (a b) -> p a b", b=128)[:, 0::4, :]
                            tt(scv, scv, Tboth.rearrange("p (a b) -> p a b", b=128),
                               ALU.add)
                        else:
                            tt(g_["SC"][:, g_["width"] - 128:g_["width"]],
                               g_["SC"][:, g_["width"] - 128:g_["width"]],
                               Tsame, ALU.add)
                    while pend:
                        emit_tail(*pend.pop(0))
                    for g_ in grp:
                        red(negm[g_["z"] % 2], g_["SC"][:, 0:g_["width"]],
                            ALU.max, negate=True)
                    for g_ in grp:
                        act(E_raw2[g_["z"] % 2][:, 0:g_["width"]],
                            g_["SC"][:, 0:g_["width"]], AF.Exp,
                            bias=negm[g_["z"] % 2], accum=den_i[g_["z"] % 2])
                    for g_ in grp:
                        act(negm[g_["z"] % 2], den_i[g_["z"] % 2], AF.Ln)
                    for g_ in grp:
                        act(rden[g_["z"] % 2], negm[g_["z"] % 2], AF.Exp, scale=-1.0)
                    for g_ in grp:
                        ts(En_i[g_["z"] % 2][:, 0:g_["width"]],
                           E_raw2[g_["z"] % 2][:, 0:g_["width"]],
                           rden[g_["z"] % 2], None, ALU.mult)
                    for g_ in grp:
                        pend.append((g_["i"], g_["qc"], g_["z"]))
                while pend:
                    emit_tail(*pend.pop(0))
                if p + 2 < 4:
                    dma(wqb[p % 2], dw["wqkv_qk"][l][p + 2], "w")
                    dma(wkb[p % 2], dw["wqkv_qk"][l][4 + p + 2], "w")
            for gq in range(2):
                dma(woib[gq], dw["wo_i"][l][gq], "w")
            for gq in range(DCH):
                for sl in range(4):
                    ps = (SCO_A, SCO_B)[sl % 2][:, 0:512]
                    for c in range(4):
                        mm(ps, woib[gq % 2][:, c * 128:(c + 1) * 128],
                           ctxT_i[c][:, sl * 512:(sl + 1) * 512], c == 0, c == 3)
                    dst = xfT[gq][:, 1 + sl * 512:1 + (sl + 1) * 512]
                    tt(dst, dst, ps, ALU.add)
                if gq + 2 < DCH:
                    dma(woib[gq % 2], dw["wo_i"][l][gq + 2], "w")
            for c in range(DCH):
                cp(sv13[:, c:c + 1], xfT[c][:, 1024:1025])
            for ms in range(2):
                for sl in (2 * ms, 2 * ms + 1):
                    restage(sl)
                    if sl == 2:
                        for c in range(4, DCH):
                            cp(stg[c][:, 0:1], sv13[:, c:c + 1])
                    for c in range(DCH):
                        mm(SCO_A[0:1, 0:512], ones128, stg_sh(c, 0, 512), c == 0, c == 7)
                    for c in range(DCH):
                        act(sqb_i[:, 0:512], stg_sh(c, 0, 512), AF.Square)
                        mm(SCO_B[0:1, 0:512], ones128, sqb_i[:, 0:512], c == 0, c == 7)
                    ts(mu_s, SCO_A[0:1, 0:512], 1.0 / D, None, ALU.mult)
                    tt(rs_s, mu_s, mu_s, ALU.mult)
                    stt(trow, SCO_B[0:1, 0:512], 1.0 / D, rs_s, ALU.mult, ALU.subtract)
                    ts(rs_s, trow, 1e-5, None, ALU.add)
                    act(trow, rs_s, AF.Ln)
                    act(rs_s, trow, AF.Exp, scale=-0.5)
                    mm(AUX[:, :], onesf_1x128, mu_s, True, True)
                    cp(ag_sb, AUX, eng="scalar")
                    mm(AUX[:, :], onesf_1x128, rs_s, True, True)
                    cp(h0_sb[:, 512:1024], AUX, eng="scalar")
                    for c in range(DCH):
                        tt(TMP, xf_sh(c, sl), ag_sb, ALU.subtract)
                        tt(h0_sb[:, 0:512], TMP, h0_sb[:, 512:1024], ALU.mult)
                        ts(hnT[c][:, (sl % 2) * 512:(sl % 2 + 1) * 512], h0_sb[:, 0:512],
                           lng_sb[:, c:c + 1], lnb_sb[:, c:c + 1], ALU.mult, ALU.add)
                    if sl == 2:
                        for c in range(4, DCH):
                            tt(TMP[:, 0:1], sv13[:, c:c + 1], ag_sb[:, 0:1], ALU.subtract)
                            tt(h0_sb[:, 0:1], TMP[:, 0:1], h0_sb[:, 512:513], ALU.mult)
                            ts(hnT[c][:, 0:1], h0_sb[:, 0:1],
                               lng_sb[:, c:c + 1], lnb_sb[:, c:c + 1], ALU.mult, ALU.add)
                dma(wkb[0], dw["w1a_t"][l][0], "w")
                dma(wvb[0], dw["w1g_t"][l][0], "w")
                for u in range(NU):
                    if u + 1 < NU:
                        dma(wkb[(u + 1) % 2], dw["w1a_t"][l][u + 1], "w")
                        dma(wvb[(u + 1) % 2], dw["w1g_t"][l][u + 1], "w")
                    for hf in range(2):
                        lt = slice(hf * 512, (hf + 1) * 512)
                        pa = SCO_A[:, hf * 512:(hf + 1) * 512]
                        pg = SCO_B[:, hf * 512:(hf + 1) * 512]
                        for c in range(DCH):
                            mm(pa, wkb[u % 2][:, c * 128:(c + 1) * 128],
                               hnT[c][:, lt], c == 0, c == 7)
                        for c in range(DCH):
                            mm(pg, wvb[u % 2][:, c * 128:(c + 1) * 128],
                               hnT[c][:, lt], c == 0, c == 7)
                        ts(ag_bf, pa, b1a_sb[:, u:u + 1], None, ALU.add)
                        act(gg_sb, pg, AF.Gelu, bias=b1g_sb[:, u:u + 1])
                        tt(h2T[u][:, hf * 512:(hf + 1) * 512], ag_bf, gg_sb, ALU.mult)
                dma(w2b[0], dw["w2_t"][l][0][:, 0:1408], "w")
                dma(w2b[1], dw["w2_t"][l][0][:, 1408:2816], "w")
                for gq in range(DCH):
                    for hf in range(2):
                        for ub in range(2):
                            for jj_ in range(11):
                                u = ub * 11 + jj_
                                mm(CTX, w2b[ub][:, jj_ * 128:(jj_ + 1) * 128],
                                   h2T[u][:, hf * 512:(hf + 1) * 512],
                                   u == 0, u == 21)
                        dst = xfT[gq][:, 1 + ms * 1024 + hf * 512:
                                      1 + ms * 1024 + (hf + 1) * 512]
                        ts(ag_sb, CTX, b2_sb[:, gq:gq + 1], None, ALU.add)
                        tt(dst, dst, ag_sb, ALU.add)
                    if gq + 1 < DCH:
                        dma(w2b[0], dw["w2_t"][l][gq + 1][:, 0:1408], "w")
                        dma(w2b[1], dw["w2_t"][l][gq + 1][:, 1408:2816], "w")
            if debug:
                dmadbg = dbg_tensor(f"xf_l{l}", (128, N + 1), F32)
                dma(dmadbg, xfT[0][:, :], "io")

        for t in range(NCH):
            xr = x1r_sb[t % 2]
            dma(xr, x1s_d[t * 128:(t + 1) * 128, :], "io")
            ob = outr_sb[t % 2]
            for c in range(DCH):
                slot = SCO_B[:, (c % 8) * 128:(c % 8) * 128 + 128]
                transp(slot, xfT[c][:, 1 + t * 128:1 + (t + 1) * 128], idf)
                tt(ob[:, c * 128:(c + 1) * 128], xr[:, c * 128:(c + 1) * 128],
                   slot, ALU.add)
            dma(out_d[t * 128:(t + 1) * 128, :], ob, "io")

        P.finalize()
        with nc.Block() as block:
            @block.tensor
            def _(e):
                P.emit("tensor", e, sems, dsems)

            @block.vector
            def _(e):
                P.emit("vector", e, sems, dsems)

            @block.scalar
            def _(e):
                P.emit("scalar", e, sems, dsems)

            @block.gpsimd
            def _(e):
                P.emit("gpsimd", e, sems, dsems)

            @block.sync
            def _(e):
                P.emit("sync", e, sems, dsems)
                for k, v in sorted(P.final_dma.items()):
                    e.wait_ge(dsems[k], v)
    return nc, dbg_out



_CACHED = {}

def _get_program(debug=False):
    key = bool(debug)
    if key not in _CACHED:
        _CACHED[key] = build_program(debug=debug)
    return _CACHED[key]


def kernel(**inputs):
    consts = host_constants()
    wl = host_weights(inputs)
    nc, _ = _get_program(debug=False)
    base = {("c_" + k): np.ascontiguousarray(v) for k, v in consts.items()}
    base.update({k: np.ascontiguousarray(v.astype(np.float32)) for k, v in wl.items()})
    X = inputs["L0_attn_hidden_states"]
    in_maps = []
    for b in range(8):
        m = dict(base)
        m["x"] = np.ascontiguousarray(X[b])
        in_maps.append(m)
    res = run_bass_kernel_spmd(nc, in_maps, core_ids=list(range(8)))
    out = np.stack([res.results[b]["out"] for b in range(8)], 0)
    return out.astype(np.float32)
